# version 09
# baseline (speedup 1.0000x reference)
import numpy as np
import ml_dtypes
import concourse.bass as bass
import concourse.mybir as mybir
from concourse.bass_utils import run_bass_kernel_spmd
from contextlib import ExitStack

F32 = mybir.dt.float32
F32R = mybir.dt.float32r
BF16 = mybir.dt.bfloat16
AF = mybir.ActivationFunctionType
ALU = mybir.AluOpType

B, S, D = 2, 8192, 1024
HID = 2048
DH = 4096
NCORE = 8
TPC = 2048
NTT = TPC // 128
NS = TPC // 512
KD = D // 128
KH = HID // 128
KDH = DH // 128
HT = TPC // 2


def build_nc(dbg=False):
    nc = bass.Bass("TRN2", target_bir_lowering=False, debug=False,
                   detect_race_conditions=False)
    dk = {"kind": "ExternalOutput"} if dbg else {}

    xs_d = nc.dram_tensor("xs", [TPC, D], F32, kind="ExternalInput")
    wfwd_d = nc.dram_tensor("wfwd", [3 * KH, 128, D], F32R, kind="ExternalInput")
    wout_d = nc.dram_tensor("wout", [KD, 128, HID], F32, kind="ExternalInput")
    w1_d = nc.dram_tensor("w1", [KDH, 128, D], F32, kind="ExternalInput")
    w2_d = nc.dram_tensor("w2", [KDH, 128, D], F32, kind="ExternalInput")
    h1_d = nc.dram_tensor("h1", [KDH, 128, DH], F32, kind="ExternalInput")
    h2_d = nc.dram_tensor("h2", [KDH, 128, DH], F32, kind="ExternalInput")
    w3_d = nc.dram_tensor("w3", [KD, 128, DH], F32, kind="ExternalInput")
    s1_d = nc.dram_tensor("s1c", [128, KDH], F32, kind="ExternalInput")
    s2_d = nc.dram_tensor("s2c", [128, KDH], F32, kind="ExternalInput")
    h1c_d = nc.dram_tensor("h1c", [128, KDH], F32, kind="ExternalInput")
    h2c_d = nc.dram_tensor("h2c", [128, KDH], F32, kind="ExternalInput")
    w3c_d = nc.dram_tensor("w3c", [128, KD], F32, kind="ExternalInput")
    cg_d = nc.dram_tensor("cg", [128, 16], F32, kind="ExternalInput")
    out_d = nc.dram_tensor("out", [TPC, D], F32, kind="ExternalOutput")

    hhat_dr = nc.dram_tensor("hhat_dr", [HID, TPC], BF16, **dk)
    ahat_dr = nc.dram_tensor("ahat_dr", [HID, TPC], BF16, **dk)
    x2_dr = nc.dram_tensor("x2_dr", [TPC, D], F32, **dk)
    xnt_dr = nc.dram_tensor("xnt_dr", [D, TPC], BF16, **dk)
    ffn_dr = nc.dram_tensor("ffn_dr", [D, TPC], BF16, **dk)
    contrib = nc.dram_tensor("contrib", [256, 16], F32)
    gath = nc.dram_tensor("gath", [2048, 16], F32)
    if dbg:
        x1n_dbg = nc.dram_tensor("x1n_dbg", [D, TPC], F32R, kind="ExternalOutput")
        u_dbg = nc.dram_tensor("u_dbg", [128, TPC], F32, kind="ExternalOutput")
        w_dbg = nc.dram_tensor("w_dbg", [128, TPC], F32, kind="ExternalOutput")
        f_dbg = nc.dram_tensor("f_dbg", [128, TPC], F32, kind="ExternalOutput")
        v_dbg = nc.dram_tensor("v_dbg", [128, TPC], F32, kind="ExternalOutput")
        st_dbg = nc.dram_tensor("st_dbg", [128, TPC], F32, kind="ExternalOutput")
        hp_dbg = nc.dram_tensor("hp_dbg", [128, TPC], F32, kind="ExternalOutput")

    with ExitStack() as OUT:
        sb = lambda n, shp, dt: OUT.enter_context(nc.sbuf_tensor(n, shp, dt))
        sem = lambda n: OUT.enter_context(nc.semaphore(n))

        ident_b = sb("ident_b", [128, 128], BF16)
        eps_t = sb("eps_t", [128, 1], F32)
        half_t = sb("half_t", [128, 1], F32)
        one_t = sb("one_t", [128, 1], F32)
        nc.const_aps.register(F32, 0.5, half_t[:, 0:1])
        nc.const_aps.register(F32, 1.0, one_t[:, 0:1])
        nc.const_aps.register(F32, 1e-5, eps_t[:, 0:1])
        sums = sb("sums", [128, 16], F32)
        sumsqs = sb("sumsqs", [128, 16], F32)
        meanb = sb("meanb", [128, 16], F32)
        m2b = sb("m2b", [128, 16], F32)
        varb = sb("varb", [128, 16], F32)
        rstdb = sb("rstdb", [128, 16], F32)
        sq_scr = sb("sq_scr", [128, 1024], F32)
        Hc = sb("Hc", [128, 16], F32)
        s1_sb = sb("s1_sb", [128, KDH], F32)
        s2_sb = sb("s2_sb", [128, KDH], F32)
        h1c_sb = sb("h1c_sb", [128, KDH], F32)
        h2c_sb = sb("h2c_sb", [128, KDH], F32)
        w3c_sb = sb("w3c_sb", [128, KD], F32)

        s_setup = sem("s_setup")
        sgv = sem("sgv")
        sga = sem("sga")
        s_xld = sem("s_xld")
        s_lnsum = sem("s_lnsum")
        s_lnsq = sem("s_lnsq")
        s_lnvar = sem("s_lnvar")
        s_lnrstd = sem("s_lnrstd")
        s_lnnorm = sem("s_lnnorm")
        s_p1tr = sem("s_p1tr")
        s_p1evac = sem("s_p1evac")
        s_wg2 = [sem(f"s_wg2{i}") for i in range(2)]
        s_xt = [sem(f"s_xt{i}") for i in range(4)]
        s_pemm = sem("s_pemm")
        s_aevac = sem("s_aevac")
        s_uw = sem("s_uw")
        s_sgm = sem("s_sgm")
        s_di = sem("s_di")
        s_pv = sem("s_pv")
        s_dst = sem("s_dst")
        s_actr = sem("s_actr")
        s_delem = sem("s_delem")
        s_dscan = sem("s_dscan")
        s_spill2 = [sem(f"s_spill2{i}") for i in range(2)]
        s_ctb = sem("s_ctb")
        s_cc = sem("s_cc")
        s_gth = sem("s_gth")
        s_H = sem("s_H")
        s_wout2 = [sem(f"s_wout2{i}") for i in range(2)]
        s_pewo = sem("s_pewo")
        phA = sem("phA")
        phB = sem("phB")
        s_hh2 = [sem(f"s_hh2{i}") for i in range(3)]
        s_dapply = sem("s_dapply")
        s_awevac = sem("s_awevac")
        s_petr = sem("s_petr")
        s_atrevac = sem("s_atrevac")
        s_dx2 = sem("s_dx2")
        s_xsp = [sem(f"s_xsp{i}") for i in range(4)]
        s_x2l = [sem(f"s_x2l{i}") for i in range(3)]
        s_xr2 = [sem(f"s_xr2{i}") for i in range(4)]
        s_pelntr = sem("s_pelntr")
        s_alnevac = sem("s_alnevac")
        s_xnsp = sem("s_xnsp")
        s_xh = sem("s_xh")
        s_wf2 = [sem(f"s_wf2{i}") for i in range(2)]
        s_pef = sem("s_pef")
        s_afe = sem("s_afe")
        s_dh = sem("s_dh")
        s_w3f2 = [sem(f"s_w3f2{i}") for i in range(2)]
        s_pe3 = sem("s_pe3")
        s_a3e = sem("s_a3e")
        s_pe3tr = sem("s_pe3tr")
        s_a3te = sem("s_a3te")
        s_x2r = sem("s_x2r")
        s_dout = sem("s_dout")
        s_out2 = [sem(f"s_out2{i}") for i in range(2)]
        s_fsp2 = [sem(f"s_fsp2{i}") for i in range(2)]
        s_ffr = sem("s_ffr")
        phC = sem("phC")

        with ExitStack() as SA:
            sbA = lambda n, shp, dt: SA.enter_context(nc.sbuf_tensor(n, shp, dt))
            psA = lambda n, shp, dt: SA.enter_context(nc.psum_tensor(n, shp, dt))
            iota_f = sbA("iota_f", [128, 128], F32)
            iota_p = sbA("iota_p", [128, 1], F32)
            ident_r = sbA("ident_r", [128, 128], F32)
            gA_sb = sbA("gA_sb", [128, 128], F32)
            gB_sb = sbA("gB_sb", [128, 128], F32)
            tmpA = sbA("tmpA", [128, 16], F32)
            tmpB = sbA("tmpB", [128, 16], F32)
            cg_sb = sbA("cg_sb", [128, 16], F32)
            btot = sbA("btot", [128, 16], F32)
            atot = sbA("atot", [128, 16], F32)
            xbuf = [sbA(f"xbuf{i}", [128, D], F32) for i in range(4)]
            xn = [sbA(f"xn{i}", [128, D], F32) for i in range(2)]
            x1nT = [sbA(f"x1nT{j}", [128, TPC], F32R) for j in range(KD)]
            u_sl = [sbA(f"u_sl{p}", [128, TPC], F32) for p in range(2)]
            w_sl = [sbA(f"w_sl{p}", [128, TPC], F32) for p in range(2)]
            hp_sl = sbA("hp_sl", [128, TPC], F32)
            eh_sl = sbA("eh_sl", [128, TPC], F32)
            st_sl = sbA("st_sl", [128, TPC], F32)
            f_sl = sbA("f_sl", [128, TPC], F32)
            v_sl = sbA("v_sl", [128, TPC], F32)
            hhat_sb = [sbA(f"hhat_sb{p}", [128, TPC], BF16) for p in range(2)]
            ahat_sb = [sbA(f"ahat_sb{p}", [128, TPC], BF16) for p in range(2)]
            wslab = [[sbA(f"wsl{p}_{g}", [128, D], F32R) for g in range(3)]
                     for p in range(2)]
            psP = [psA(f"psP{p}", [128, 512], F32) for p in range(2)]
            psG = [[psA(f"psG{g}_{p}", [128, 512], F32) for p in range(2)]
                   for g in range(3)]

            with nc.Block("b1") as bl:

                @bl.sync
                def _(sp):
                    sp.dma_start(out=cg_sb[:, :], in_=cg_d[:, :]).then_inc(s_xld, 16)
                    sp.dma_start(out=s1_sb[:, :], in_=s1_d[:, :]).then_inc(s_xld, 16)
                    sp.dma_start(out=s2_sb[:, :], in_=s2_d[:, :]).then_inc(s_xld, 16)
                    sp.dma_start(out=h1c_sb[:, :], in_=h1c_d[:, :]).then_inc(s_xld, 16)
                    sp.dma_start(out=h2c_sb[:, :], in_=h2c_d[:, :]).then_inc(s_xld, 16)
                    sp.dma_start(out=w3c_sb[:, :], in_=w3c_d[:, :]).then_inc(s_xld, 16)
                    for k in range(2):
                        for g3 in range(3):
                            sp.dma_start(out=wslab[k % 2][g3][:, :],
                                         in_=wfwd_d[g3 * KH + k]).then_inc(s_wg2[k % 2], 16)
                    for i in range(NTT):
                        if i >= 4:
                            sp.wait_ge(s_lnnorm, i - 3)
                        sp.dma_start(out=xbuf[i % 4][:, :],
                                     in_=xs_d[i * 128:(i + 1) * 128, :]).then_inc(s_xt[i % 4], 16)
                    for k in range(2, KH):
                        sp.wait_ge(s_pemm, 4 * (k - 1))
                        for g3 in range(3):
                            sp.dma_start(out=wslab[k % 2][g3][:, :],
                                         in_=wfwd_d[g3 * KH + k]).then_inc(s_wg2[k % 2], 16)
                    sp.sem_inc(phA, 1)
                    sp.wait_ge(phA, 5)

                @bl.gpsimd
                def _(g):
                    g.memset(eps_t[:, :], 1e-5)
                    g.memset(half_t[:, :], 0.5)
                    g.memset(one_t[:, :], 1.0)
                    g.iota(iota_f[:, :], [[1, 128]], channel_multiplier=0,
                           allow_small_or_imprecise_dtypes=True)
                    g.iota(iota_p[:, :], [[1, 1]], channel_multiplier=1,
                           allow_small_or_imprecise_dtypes=True)
                    g.drain().then_inc(s_setup, 1)
                    def _spill_ha(kk):
                        g.wait_ge(s_dscan, kk + 1)
                        g.dma_start(out=hhat_dr[kk * 128:(kk + 1) * 128, :],
                                    in_=hhat_sb[kk % 2][:, :]).then_inc(s_spill2[kk % 2], 16)
                        g.dma_start(out=ahat_dr[kk * 128:(kk + 1) * 128, :],
                                    in_=ahat_sb[kk % 2][:, :]).then_inc(s_spill2[kk % 2], 16)
                    for k in range(KH):
                        g.wait_ge(s_di, k + 1)
                        g.tensor_tensor(v_sl[:, :], u_sl[k % 2][:, :], hp_sl[:, :],
                                        ALU.mult).then_inc(s_pv, 1)
                        g.sem_inc(s_delem, 1)
                        if k >= 1:
                            _spill_ha(k - 1)
                    _spill_ha(KH - 1)
                    if dbg:
                        for jb in range(KD):
                            g.dma_start(out=x1n_dbg[jb * 128:(jb + 1) * 128, :],
                                        in_=x1nT[jb][:, :]).then_inc(s_ctb, 16)
                        for dbt, dbs in [(u_dbg, u_sl), (w_dbg, w_sl),
                                         (f_dbg, f_sl), (v_dbg, v_sl),
                                         (st_dbg, st_sl), (hp_dbg, hp_sl)]:
                            g.dma_start(out=dbt[:, :], in_=dbs[:, :]).then_inc(s_ctb, 16)
                        g.wait_ge(s_ctb, 16 * 14)
                    g.dma_start(out=contrib[0:128, :], in_=atot[:, :]).then_inc(s_ctb, 16)
                    g.dma_start(out=contrib[128:256, :], in_=btot[:, :]).then_inc(s_ctb, 16)
                    g.wait_ge(s_ctb, (16 * 14 if dbg else 0) + 32)
                    g.collective_compute(
                        "AllGather", ALU.bypass,
                        replica_groups=[list(range(NCORE))],
                        ins=[contrib.ap().opt()],
                        outs=[gath.ap().opt()],
                    ).then_inc(s_cc, 1)
                    g.wait_ge(s_cc, 1)
                    g.dma_start(out=gA_sb[:, :],
                                in_=bass.AP(gath, 0, [[16, 128], [4096, 8],
                                                      [1, 16]])).then_inc(s_gth, 16)
                    g.dma_start(out=gB_sb[:, :],
                                in_=bass.AP(gath, 2048, [[16, 128], [4096, 8],
                                                         [1, 16]])).then_inc(s_gth, 16)
                    g.wait_ge(s_spill2[0], 32 * (KH // 2))
                    g.wait_ge(s_spill2[1], 32 * (KH // 2))
                    g.sem_inc(phA, 1)
                    g.wait_ge(phA, 5)

                @bl.vector
                def _(v):
                    vg = [0]

                    def vguard():
                        vg[0] += 1
                        v.drain().then_inc(sgv, 1)
                        v.wait_ge(sgv, vg[0])

                    v.wait_ge(s_setup, 1)
                    v.tensor_scalar(ident_b[:, :], iota_f[:, :], iota_p[:, :], None,
                                    ALU.is_equal)
                    v.memset(Hc[:, :], 0.0).then_inc(s_setup, 1)
                    for grp in range(4):
                        for j in range(4):
                            i = 4 * grp + j
                            v.wait_ge(s_xt[i % 4], 16 * (i // 4 + 1))
                            ins = v.tensor_reduce(sums[:, i:i + 1], xbuf[i % 4][:, :],
                                                  mybir.AxisListType.X, ALU.add)
                        ins.then_inc(s_lnsum, 1)
                        v.wait_ge(s_lnsq, grp + 1)
                        c0, c1 = 4 * grp, 4 * grp + 4
                        vguard()
                        v.tensor_scalar(meanb[:, c0:c1], sums[:, c0:c1],
                                        1.0 / D, None, ALU.mult)
                        vguard()
                        v.tensor_tensor(m2b[:, c0:c1], meanb[:, c0:c1],
                                        meanb[:, c0:c1], ALU.mult)
                        vguard()
                        v.scalar_tensor_tensor(varb[:, c0:c1], sumsqs[:, c0:c1],
                                               1.0 / D, m2b[:, c0:c1],
                                               ALU.mult, ALU.subtract).then_inc(s_lnvar, 1)
                        v.wait_ge(s_lnrstd, grp + 1)
                        for j in range(4):
                            i = 4 * grp + j
                            if i >= 2:
                                v.wait_ge(s_p1tr, 8 * (i - 1))
                            v.tensor_scalar(xn[i % 2][:, :], xbuf[i % 4][:, :],
                                            meanb[:, i:i + 1], rstdb[:, i:i + 1],
                                            ALU.subtract, ALU.mult).then_inc(s_lnnorm, 1)
                    for k in range(KH):
                        v.wait_ge(s_uw, 4 * (k + 1))
                        v.scalar_tensor_tensor(st_sl[:, :], u_sl[k % 2][:, :], 2.0,
                                               w_sl[k % 2][:, :],
                                               ALU.add, ALU.add).then_inc(s_dst, 1)
                        v.wait_ge(s_actr, k + 1)
                        v.scalar_tensor_tensor(f_sl[:, :], w_sl[k % 2][:, :], 1.0,
                                               st_sl[:, :], ALU.add, ALU.mult)
                        v.scalar_tensor_tensor(u_sl[k % 2][:, :], u_sl[k % 2][:, :],
                                               1.0, st_sl[:, :],
                                               ALU.add, ALU.mult)
                        v.wait_ge(s_sgm, k + 1)
                        v.tensor_tensor(hp_sl[:, :], hp_sl[:, :], eh_sl[:, :],
                                        ALU.max).then_inc(s_di, 1)
                        v.wait_ge(s_pv, k + 1)
                        if k >= 2:
                            v.wait_ge(s_spill2[k % 2], 32 * (k // 2))
                        v.tensor_tensor_scan(hhat_sb[k % 2][:, :], f_sl[:, :],
                                             v_sl[:, :], 0.0, ALU.mult, ALU.add)
                        v.tensor_tensor_scan(ahat_sb[k % 2][:, :], f_sl[:, :],
                                             f_sl[:, :], 1.0, ALU.mult, ALU.bypass)
                        vguard()
                        v.tensor_copy(btot[:, k:k + 1], hhat_sb[k % 2][:, TPC - 1:TPC])
                        v.tensor_copy(atot[:, k:k + 1],
                                      ahat_sb[k % 2][:, TPC - 1:TPC]).then_inc(s_dscan, 1)
                    v.wait_ge(s_xld, 96)
                    v.wait_ge(s_gth, 32)
                    for r in range(NCORE):
                        v.tensor_scalar(tmpA[:, :], gA_sb[:, r * 16:(r + 1) * 16],
                                        cg_sb[:, r:r + 1], cg_sb[:, 8 + r:9 + r],
                                        ALU.mult, ALU.add)
                        vguard()
                        v.tensor_tensor(Hc[:, :], Hc[:, :], tmpA[:, :], ALU.mult)
                        v.tensor_scalar(tmpB[:, :], gB_sb[:, r * 16:(r + 1) * 16],
                                        cg_sb[:, r:r + 1], None, ALU.mult)
                        vguard()
                        ins = v.tensor_tensor(Hc[:, :], Hc[:, :], tmpB[:, :], ALU.add)
                    ins.then_inc(s_H, 1)
                    v.drain().then_inc(phA, 1)
                    v.wait_ge(phA, 5)

                @bl.scalar
                def _(sc):
                    ag = [0]

                    def aguard():
                        ag[0] += 1
                        sc.drain().then_inc(sga, 1)
                        sc.wait_ge(sga, ag[0])

                    sc.wait_ge(s_setup, 2)
                    sc.activation(ident_r[:, :], ident_b[:, :],
                                  AF.Copy).then_inc(s_setup, 1)
                    for grp in range(4):
                        for j in range(4):
                            i = 4 * grp + j
                            sc.wait_ge(s_xt[i % 4], 16 * (i // 4 + 1))
                            ins = sc.activation(sq_scr[:, :], xbuf[i % 4][:, :],
                                                AF.Square,
                                                accum_out=sumsqs[:, i:i + 1])
                        ins.then_inc(s_lnsq, 1)
                        sc.wait_ge(s_lnvar, grp + 1)
                        c0, c1 = 4 * grp, 4 * grp + 4
                        sc.activation(varb[:, c0:c1], varb[:, c0:c1], AF.Ln,
                                      bias=eps_t[:, :])
                        aguard()
                        sc.activation(rstdb[:, c0:c1], varb[:, c0:c1], AF.Exp,
                                      scale=-0.5).then_inc(s_lnrstd, 1)
                        for j in range(4):
                            i = 4 * grp + j
                            for half in range(2):
                                q = 2 * i + half
                                sc.wait_ge(s_p1tr, 4 * (q + 1))
                                for jj in range(4):
                                    jb = half * 4 + jj
                                    sc.copy(x1nT[jb][:, i * 128:(i + 1) * 128],
                                            psP[q % 2][:, jj * 128:(jj + 1) * 128]
                                            ).then_inc(s_p1evac, 1)
                    for k in range(KH):
                        def _uw(s):
                            gidx = 4 * k + s
                            par = gidx % 2
                            sc.wait_ge(s_pemm, gidx + 1)
                            cs = slice(s * 512, (s + 1) * 512)
                            sc.activation(u_sl[k % 2][:, cs], psG[0][par][:, :],
                                          AF.Exp, scale=-1.0)
                            sc.activation(w_sl[k % 2][:, cs], psG[1][par][:, :],
                                          AF.Exp, scale=-1.0).then_inc(s_uw, 1)

                        def _ehhp(s):
                            gidx = 4 * k + s
                            par = gidx % 2
                            cs = slice(s * 512, (s + 1) * 512)
                            sc.activation(eh_sl[:, cs], psG[2][par][:, :], AF.Exp,
                                          scale=-1.0)
                            sc.activation(hp_sl[:, cs], psG[2][par][:, :], AF.Copy,
                                          bias=0.5).then_inc(s_aevac, 1)

                        if k >= 2:
                            sc.wait_ge(s_di, k - 1)
                            sc.wait_ge(s_delem, k - 1)
                        _uw(0)
                        _uw(1)
                        if k >= 1:
                            sc.wait_ge(s_delem, k)
                        _ehhp(0)
                        _uw(2)
                        _ehhp(1)
                        _uw(3)
                        _ehhp(2)
                        _ehhp(3)
                        sc.activation(eh_sl[:, :], eh_sl[:, :], AF.Ln,
                                      bias=one_t[:, :])
                        sc.activation(eh_sl[:, :], eh_sl[:, :], AF.Exp,
                                      scale=-1.0).then_inc(s_sgm, 1)
                        sc.wait_ge(s_dst, k + 1)
                        sc.activation(st_sl[:, :], st_sl[:, :], AF.Ln)
                        sc.activation(st_sl[:, :], st_sl[:, :], AF.Exp,
                                      scale=-1.0).then_inc(s_actr, 1)
                    sc.drain().then_inc(phA, 1)
                    sc.wait_ge(phA, 5)

                @bl.tensor
                def _(t):
                    t.wait_ge(s_setup, 3)
                    for i in range(NTT):
                        t.wait_ge(s_lnnorm, i + 1)
                        for jb in range(KD):
                            q = 2 * i + (jb // 4)
                            if jb % 4 == 0 and q >= 2:
                                t.wait_ge(s_p1evac, 4 * (q - 1))
                            t.transpose(psP[q % 2][:, (jb % 4) * 128:(jb % 4 + 1) * 128],
                                        xn[i % 2][:, jb * 128:(jb + 1) * 128],
                                        ident_r[:, :]).then_inc(s_p1tr, 1)
                    for k in range(KH):
                        for s in range(NS):
                            gidx = 4 * k + s
                            par = gidx % 2
                            if gidx == 0:
                                t.wait_ge(s_p1evac, 128)
                            if s == 0:
                                t.wait_ge(s_wg2[k % 2], 48 * (k // 2 + 1))
                            if gidx >= 2:
                                t.wait_ge(s_uw, gidx - 1)
                                t.wait_ge(s_aevac, gidx - 1)
                            for g3 in range(3):
                                for kk in range(KD):
                                    ins = t.matmul(
                                        psG[g3][par][:, :],
                                        wslab[k % 2][g3][:, kk * 128:(kk + 1) * 128],
                                        x1nT[kk][:, s * 512:(s + 1) * 512],
                                        start=(kk == 0), stop=(kk == KD - 1))
                            ins.then_inc(s_pemm, 1)
                    t.drain().then_inc(phA, 1)
                    t.wait_ge(phA, 5)

        with ExitStack() as SB:
            sbB = lambda n, shp, dt: SB.enter_context(nc.sbuf_tensor(n, shp, dt))
            wfb = [sbB(f"wfb{p}", [128, DH], BF16) for p in range(2)]
            with ExitStack() as SD:
                sbD = lambda n, shp, dt: SD.enter_context(nc.sbuf_tensor(n, shp, dt))
                psD = lambda n, shp, dt: SD.enter_context(nc.psum_tensor(n, shp, dt))
                woutb = [sbD(f"woutb{p}", [128, HID], BF16) for p in range(2)]
                x2nT = [sbD(f"x2nT{j}", [128, TPC], BF16) for j in range(KD)]
                hfull = [sbD(f"hfull{k}", [128, TPC], BF16) for k in range(KH)]
                hh_sb = [sbD(f"hh_sb{p}", [128, TPC], BF16) for p in range(3)]
                ah_sb = [sbD(f"ah_sb{p}", [128, TPC], BF16) for p in range(3)]
                mixT = [sbD(f"mixT{o}", [128, 512], BF16) for o in range(KD)]
                mix_tok = [sbD(f"mix_tok{j}", [128, D], BF16) for j in range(4)]
                ln2b = sbD("ln2b", [128, D], F32)
                xbuf2 = [sbD(f"xbuf2{p}", [128, D], F32) for p in range(3)]
                x2buf = [sbD(f"x2buf{j}", [128, D], F32) for j in range(4)]
                x2n = [sbD(f"x2n{p}", [128, D], BF16) for p in range(2)]
                psW = [psD(f"psW{o4}", [128, 512], F32) for o4 in range(4)]
                psT = [psD(f"psT{p}", [128, 1024], BF16) for p in range(2)]
                psL = [psD(f"psL{p}", [128, 1024], BF16) for p in range(2)]

                with nc.Block("b2") as bl:

                    @bl.sync
                    def _(sp):
                        for i in range(NTT):
                            if i >= 3:
                                sp.wait_ge(s_dx2, i - 2)
                            sp.dma_start(out=xbuf2[i % 3][:, :],
                                         in_=xs_d[i * 128:(i + 1) * 128, :]).then_inc(s_x2l[i % 3], 16)
                            sp.wait_ge(s_dx2, i + 1)
                            sp.dma_start(out=x2_dr[i * 128:(i + 1) * 128, :],
                                         in_=x2buf[i % 4][:, :]).then_inc(s_xsp[i % 4], 16)
                        for j in range(4):
                            sp.wait_ge(s_xsp[j], 16 * (NTT // 4))
                        for i in range(NTT):
                            if i >= 4:
                                sp.wait_ge(s_lnnorm, 29 + i)
                            sp.dma_start(out=x2buf[i % 4][:, :],
                                         in_=x2_dr[i * 128:(i + 1) * 128, :]).then_inc(s_xr2[i % 4], 16)
                        sp.sem_inc(phB, 1)
                        sp.wait_ge(phB, 5)

                    @bl.gpsimd
                    def _(g):
                        for ws in range(2):
                            g.dma_start(out=woutb[ws % 2][:, :],
                                        in_=wout_d[ws % KD]).then_inc(s_wout2[ws % 2], 16)
                        g.wait_ge(s_spill2[0], 32 * (KH // 2))
                        g.wait_ge(s_spill2[1], 32 * (KH // 2))
                        for k in range(KH):
                            if k >= 3:
                                g.wait_ge(s_dapply, k - 2)
                            g.dma_start(out=hh_sb[k % 3][:, :],
                                        in_=hhat_dr[k * 128:(k + 1) * 128, :]).then_inc(s_hh2[k % 3], 16)
                            g.dma_start(out=ah_sb[k % 3][:, :],
                                        in_=ahat_dr[k * 128:(k + 1) * 128, :]).then_inc(s_hh2[k % 3], 16)
                        for ws in range(2, 4 * NS * 2):
                            g.wait_ge(s_pewo, ws - 1)
                            g.dma_start(out=woutb[ws % 2][:, :],
                                        in_=wout_d[ws % KD]).then_inc(s_wout2[ws % 2], 16)
                        for o in range(2):
                            g.dma_start(out=wfb[o % 2][:, 0:D],
                                        in_=w1_d[o]).then_inc(s_wf2[o % 2], 16)
                        g.wait_ge(s_alnevac, 128)
                        for jb in range(KD):
                            g.dma_start(out=xnt_dr[jb * 128:(jb + 1) * 128, :],
                                        in_=x2nT[jb][:, :]).then_inc(s_xnsp, 16)
                        g.wait_ge(s_xnsp, 16 * KD)
                        g.sem_inc(phB, 1)
                        g.wait_ge(phB, 5)

                    @bl.vector
                    def _(v):
                        vg = [0]

                        def vguard():
                            vg[0] += 1
                            v.drain().then_inc(sgv, 1)
                            v.wait_ge(sgv, vg[0])

                        v.wait_ge(s_H, 1)
                        for k in range(KH):
                            v.wait_ge(s_hh2[k % 3], 32 * (k // 3 + 1))
                            v.scalar_tensor_tensor(hfull[k][:, :], ah_sb[k % 3][:, :],
                                                   Hc[:, k:k + 1], hh_sb[k % 3][:, :],
                                                   ALU.mult, ALU.add).then_inc(s_dapply, 1)
                        for s in range(NS):
                            v.wait_ge(s_atrevac, 4 * (s + 1))
                            for j in range(4):
                                i = 4 * s + j
                                ins = v.tensor_reduce(sums[:, i:i + 1],
                                                      mix_tok[j][:, :],
                                                      mybir.AxisListType.X, ALU.add)
                            ins.then_inc(s_lnsum, 1)
                            gidx = 4 + s
                            v.wait_ge(s_lnsq, gidx + 1)
                            c0, c1 = 4 * s, 4 * s + 4
                            vguard()
                            v.tensor_scalar(meanb[:, c0:c1], sums[:, c0:c1],
                                            1.0 / D, None, ALU.mult)
                            vguard()
                            v.tensor_tensor(m2b[:, c0:c1], meanb[:, c0:c1],
                                            meanb[:, c0:c1], ALU.mult)
                            vguard()
                            v.scalar_tensor_tensor(varb[:, c0:c1], sumsqs[:, c0:c1],
                                                   1.0 / D, m2b[:, c0:c1],
                                                   ALU.mult, ALU.subtract
                                                   ).then_inc(s_lnvar, 1)
                            v.wait_ge(s_lnrstd, gidx + 1)
                            if s >= 1:
                                for j in range(4):
                                    v.wait_ge(s_xsp[j], 16 * s)
                            for j in range(4):
                                i = 4 * s + j
                                v.tensor_scalar(ln2b[:, :], mix_tok[j][:, :],
                                                meanb[:, i:i + 1], rstdb[:, i:i + 1],
                                                ALU.subtract,
                                                ALU.mult).then_inc(s_lnnorm, 1)
                                v.wait_ge(s_x2l[i % 3], 16 * (i // 3 + 1))
                                v.tensor_tensor(x2buf[j][:, :], xbuf2[i % 3][:, :],
                                                ln2b[:, :], ALU.add).then_inc(s_dx2, 1)
                        for gi in range(4):
                            for j in range(4):
                                i = 4 * gi + j
                                v.wait_ge(s_xr2[i % 4], 16 * (i // 4 + 1))
                                ins = v.tensor_reduce(sums[:, i:i + 1],
                                                      x2buf[i % 4][:, :],
                                                      mybir.AxisListType.X, ALU.add)
                            ins.then_inc(s_lnsum, 1)
                            gidx = 8 + gi
                            v.wait_ge(s_lnsq, gidx + 1)
                            c0, c1 = 4 * gi, 4 * gi + 4
                            vguard()
                            v.tensor_scalar(meanb[:, c0:c1], sums[:, c0:c1],
                                            1.0 / D, None, ALU.mult)
                            vguard()
                            v.tensor_tensor(m2b[:, c0:c1], meanb[:, c0:c1],
                                            meanb[:, c0:c1], ALU.mult)
                            vguard()
                            v.scalar_tensor_tensor(varb[:, c0:c1], sumsqs[:, c0:c1],
                                                   1.0 / D, m2b[:, c0:c1],
                                                   ALU.mult, ALU.subtract
                                                   ).then_inc(s_lnvar, 1)
                            v.wait_ge(s_lnrstd, gidx + 1)
                            for j in range(4):
                                i = 4 * gi + j
                                if i >= 2:
                                    v.wait_ge(s_pelntr, 8 * (i - 1))
                                v.tensor_scalar(x2n[i % 2][:, :], x2buf[i % 4][:, :],
                                                meanb[:, i:i + 1], rstdb[:, i:i + 1],
                                                ALU.subtract,
                                                ALU.mult).then_inc(s_lnnorm, 1)
                        v.drain().then_inc(phB, 1)
                        v.wait_ge(phB, 5)

                    @bl.scalar
                    def _(sc):
                        ag = [0]

                        def aguard():
                            ag[0] += 1
                            sc.drain().then_inc(sga, 1)
                            sc.wait_ge(sga, ag[0])

                        for s in range(NS):
                            for oh in range(2):
                                idx = 2 * s + oh
                                sc.wait_ge(s_pewo, 4 * (idx + 1))
                                if s >= 1:
                                    sc.wait_ge(s_petr, 8 * (4 * s + 2 * oh - 2))
                                for o4 in range(4):
                                    ins = sc.copy(mixT[oh * 4 + o4][:, :], psW[o4][:, :])
                                ins.then_inc(s_awevac, 1)
                                for half in range(2):
                                    tq = 4 * s + 2 * oh + half
                                    sc.wait_ge(s_petr, 8 * (tq + 1))
                                    if s >= 1 and oh == 0 and half == 0:
                                        sc.wait_ge(s_lnsum, s + 4)
                                        sc.wait_ge(s_lnnorm, 16 + 4 * (s - 1) + 4)
                                    for n in range(8):
                                        nn = half * 8 + n
                                        o4 = nn // 4
                                        j = nn % 4
                                        o = oh * 4 + o4
                                        ins = sc.copy(
                                            mix_tok[j][:, o * 128:(o + 1) * 128],
                                            psT[tq % 2][:, n * 128:(n + 1) * 128])
                                    ins.then_inc(s_atrevac, 1)
                            gidx = 4 + s
                            sc.wait_ge(s_lnsum, gidx + 1)
                            for j in range(4):
                                i = 4 * s + j
                                ins = sc.activation(sq_scr[:, :], mix_tok[j][:, :],
                                                    AF.Square,
                                                    accum_out=sumsqs[:, i:i + 1])
                            ins.then_inc(s_lnsq, 1)
                            sc.wait_ge(s_lnvar, gidx + 1)
                            c0, c1 = 4 * s, 4 * s + 4
                            sc.activation(varb[:, c0:c1], varb[:, c0:c1], AF.Ln,
                                          bias=eps_t[:, :])
                            aguard()
                            sc.activation(rstdb[:, c0:c1], varb[:, c0:c1], AF.Exp,
                                          scale=-0.5).then_inc(s_lnrstd, 1)
                        for gi in range(4):
                            gidx = 8 + gi
                            sc.wait_ge(s_lnsum, gidx + 1)
                            for j in range(4):
                                i = 4 * gi + j
                                sc.wait_ge(s_xr2[i % 4], 16 * (i // 4 + 1))
                                ins = sc.activation(sq_scr[:, :], x2buf[i % 4][:, :],
                                                    AF.Square,
                                                    accum_out=sumsqs[:, i:i + 1])
                            ins.then_inc(s_lnsq, 1)
                            sc.wait_ge(s_lnvar, gidx + 1)
                            c0, c1 = 4 * gi, 4 * gi + 4
                            sc.activation(varb[:, c0:c1], varb[:, c0:c1], AF.Ln,
                                          bias=eps_t[:, :])
                            aguard()
                            sc.activation(rstdb[:, c0:c1], varb[:, c0:c1], AF.Exp,
                                          scale=-0.5).then_inc(s_lnrstd, 1)
                            for j in range(4):
                                i = 4 * gi + j
                                sc.wait_ge(s_pelntr, 8 * (i + 1))
                                for jb in range(KD):
                                    sc.copy(x2nT[jb][:, i * 128:(i + 1) * 128],
                                            psL[i % 2][:, jb * 128:(jb + 1) * 128]
                                            ).then_inc(s_alnevac, 1)
                        sc.drain().then_inc(phB, 1)
                        sc.wait_ge(phB, 5)

                    @bl.tensor
                    def _(t):
                        for s in range(NS):
                            for oh in range(2):
                                idx = 2 * s + oh
                                if idx >= 1:
                                    t.wait_ge(s_awevac, idx)
                                for o4 in range(4):
                                    o = oh * 4 + o4
                                    ws = 8 * s + o
                                    t.wait_ge(s_wout2[ws % 2], 16 * (ws // 2 + 1))
                                    for k in range(KH):
                                        if s == 0 and oh == 0 and o4 == 0:
                                            t.wait_ge(s_dapply, k + 1)
                                        ins = t.matmul(
                                            psW[o4][:, :],
                                            woutb[ws % 2][:, k * 128:(k + 1) * 128],
                                            hfull[k][:, s * 512:(s + 1) * 512],
                                            start=(k == 0), stop=(k == KH - 1))
                                    ins.then_inc(s_pewo, 1)
                                t.wait_ge(s_awevac, idx + 1)
                                for half in range(2):
                                    tq = 4 * s + 2 * oh + half
                                    if tq >= 2:
                                        t.wait_ge(s_atrevac, tq - 1)
                                    for n in range(8):
                                        nn = half * 8 + n
                                        o4 = nn // 4
                                        j = nn % 4
                                        t.transpose(
                                            psT[tq % 2][:, n * 128:(n + 1) * 128],
                                            mixT[oh * 4 + o4][:, j * 128:(j + 1) * 128],
                                            ident_b[:, :]).then_inc(s_petr, 1)
                        for i in range(NTT):
                            t.wait_ge(s_lnnorm, 32 + i + 1)
                            if i >= 2:
                                t.wait_ge(s_alnevac, 8 * (i - 1))
                            for jb in range(KD):
                                t.transpose(psL[i % 2][:, jb * 128:(jb + 1) * 128],
                                            x2n[i % 2][:, jb * 128:(jb + 1) * 128],
                                            ident_b[:, :]).then_inc(s_pelntr, 1)
                        t.drain().then_inc(phB, 1)
                        t.wait_ge(phB, 5)

            with ExitStack() as SC:
                sbC = lambda n, shp, dt: SC.enter_context(nc.sbuf_tensor(n, shp, dt))
                psC = lambda n, shp, dt: SC.enter_context(nc.psum_tensor(n, shp, dt))
                y1 = [sbC(f"y1_{o}", [128, HT], BF16) for o in range(KDH)]
                y2 = [sbC(f"y2_{o}", [128, HT], BF16) for o in range(KDH)]
                x2nTh = [sbC(f"x2nTh{j}", [128, HT], BF16) for j in range(KD)]
                w3b = [sbC(f"w3b{p}", [128, DH], BF16) for p in range(2)]
                fsb = [sbC(f"fsb{p}", [128, 512], BF16) for p in range(2)]
                psF = [[psC(f"psF{s2}_{p}", [128, 512], F32) for p in range(2)]
                       for s2 in range(2)]
                psW3 = [psC(f"psW3{p}", [128, 512], F32) for p in range(2)]

                with nc.Block("b3") as bl:

                    @bl.sync
                    def _(sp):
                        sp.sem_inc(phC, 1)
                        sp.wait_ge(phC, 5)

                    @bl.gpsimd
                    def _(g):
                        for hh in range(2):
                            if hh == 1:
                                g.wait_ge(s_pef, 64)
                            for jb in range(KD):
                                g.dma_start(out=x2nTh[jb][:, :],
                                            in_=xnt_dr[jb * 128:(jb + 1) * 128,
                                                       hh * HT:(hh + 1) * HT]).then_inc(s_xh, 16)
                            for p, wd in enumerate([w1_d, w2_d, h1_d, h2_d]):
                                width = D if p < 2 else DH
                                for o in range(KDH):
                                    GL = 128 * hh + 32 * p + o
                                    if GL < 2:
                                        continue
                                    g.wait_ge(s_pef, GL - 1)
                                    g.dma_start(out=wfb[GL % 2][:, 0:width],
                                                in_=wd[o]).then_inc(s_wf2[GL % 2], 16)
                            def _spill(m):
                                s2m = (m % 16) // 8
                                om = m % 8
                                g.wait_ge(s_a3e, m + 1)
                                cs0 = (m // 16) * HT + s2m * 512
                                g.dma_start(out=ffn_dr[om * 128:(om + 1) * 128,
                                                       cs0:cs0 + 512],
                                            in_=fsb[m % 2][:, :]).then_inc(s_fsp2[m % 2], 16)
                            for mm in range(16):
                                m = 16 * hh + mm
                                if m >= 2:
                                    g.wait_ge(s_pe3, m - 1)
                                g.dma_start(out=w3b[m % 2][:, :],
                                            in_=w3_d[m % 8]).then_inc(s_w3f2[m % 2], 16)
                                if m >= 2:
                                    _spill(m - 2)
                            _spill(16 * hh + 14)
                            _spill(16 * hh + 15)
                        g.wait_ge(s_fsp2[0], 16 * 16)
                        g.wait_ge(s_fsp2[1], 16 * 16)
                        g.sem_inc(phC, 1)
                        g.wait_ge(phC, 5)

                    @bl.vector
                    def _(v):
                        for hh in range(2):
                            for o in range(KDH):
                                v.wait_ge(s_afe, 128 * hh + 32 + o + 1)
                                v.tensor_tensor(y1[o][:, :], y1[o][:, :],
                                                y2[o][:, :], ALU.mult).then_inc(s_dh, 1)
                        v.drain().then_inc(phC, 1)
                        v.wait_ge(phC, 5)

                    @bl.scalar
                    def _(sc):
                        sc.wait_ge(s_xld, 96)
                        for hh in range(2):
                            for p in range(4):
                                for o in range(KDH):
                                    GL = 128 * hh + 32 * p + o
                                    sc.wait_ge(s_pef, GL + 1)
                                    for s2 in range(2):
                                        ps = psF[s2][GL % 2]
                                        cs = slice(s2 * 512, (s2 + 1) * 512)
                                        if p == 0:
                                            ins = sc.activation(y1[o][:, cs], ps[:, :],
                                                                AF.Silu,
                                                                scale=s1_sb[:, o:o + 1])
                                        elif p == 1:
                                            ins = sc.activation(y2[o][:, cs], ps[:, :],
                                                                AF.Copy,
                                                                scale=s2_sb[:, o:o + 1])
                                        elif p == 2:
                                            ins = sc.activation(y2[o][:, cs], ps[:, :],
                                                                AF.Gelu,
                                                                scale=h1c_sb[:, o:o + 1])
                                        else:
                                            ins = sc.activation(y1[o][:, cs], ps[:, :],
                                                                AF.Gelu,
                                                                scale=h2c_sb[:, o:o + 1])
                                    ins.then_inc(s_afe, 1)
                            for s2 in range(2):
                                for o in range(KD):
                                    m = 16 * hh + 8 * s2 + o
                                    sc.wait_ge(s_pe3, m + 1)
                                    if m >= 2:
                                        sc.wait_ge(s_fsp2[m % 2], 16 * (m // 2))
                                    sc.activation(fsb[m % 2][:, :], psW3[m % 2][:, :],
                                                  AF.Copy,
                                                  scale=w3c_sb[:, o:o + 1]
                                                  ).then_inc(s_a3e, 1)
                        sc.drain().then_inc(phC, 1)
                        sc.wait_ge(phC, 5)

                    @bl.tensor
                    def _(t):
                        for hh in range(2):
                            t.wait_ge(s_xh, 128 * (hh + 1))
                            for p in range(4):
                                if p == 2:
                                    t.wait_ge(s_dh, KDH * (hh + 1))
                                nk = KD if p < 2 else KDH
                                rhs = x2nTh if p < 2 else (y1 if p == 2 else y2)
                                for o in range(KDH):
                                    GL = 128 * hh + 32 * p + o
                                    t.wait_ge(s_wf2[GL % 2], 16 * (GL // 2 + 1))
                                    if GL >= 2:
                                        t.wait_ge(s_afe, GL - 1)
                                    for s2 in range(2):
                                        for kk in range(nk):
                                            ins = t.matmul(
                                                psF[s2][GL % 2][:, :],
                                                wfb[GL % 2][:, kk * 128:(kk + 1) * 128],
                                                rhs[kk][:, s2 * 512:(s2 + 1) * 512],
                                                start=(kk == 0), stop=(kk == nk - 1))
                                    ins.then_inc(s_pef, 1)
                            for s2 in range(2):
                                gi3 = 2 * hh + s2
                                for o in range(KD):
                                    m = 16 * hh + 8 * s2 + o
                                    if s2 == 0 and o == 0:
                                        t.wait_ge(s_afe, 128 * (hh + 1))
                                    t.wait_ge(s_w3f2[m % 2], 16 * (m // 2 + 1))
                                    if m >= 2:
                                        t.wait_ge(s_a3e, m - 1)
                                    for kk in range(KDH):
                                        ins = t.matmul(
                                            psW3[m % 2][:, :],
                                            w3b[m % 2][:, kk * 128:(kk + 1) * 128],
                                            y1[kk][:, s2 * 512:(s2 + 1) * 512],
                                            start=(kk == 0), stop=(kk == KDH - 1))
                                    ins.then_inc(s_pe3, 1)
                        t.drain().then_inc(phC, 1)
                        t.wait_ge(phC, 5)

            with ExitStack() as SE:
                sbE = lambda n, shp, dt: SE.enter_context(nc.sbuf_tensor(n, shp, dt))
                psE = lambda n, shp, dt: SE.enter_context(nc.psum_tensor(n, shp, dt))
                fft = [sbE(f"fft{o}", [128, 512], BF16) for o in range(KD)]
                ffn_tok = [sbE(f"ffn_tok{j}", [128, D], BF16) for j in range(4)]
                ln4b = sbE("ln4b", [128, D], BF16)
                x2r = [sbE(f"x2r{p}", [128, D], F32) for p in range(2)]
                out_sb = [sbE(f"out_sb{p}", [128, D], F32) for p in range(2)]
                psTT = [psE(f"psTT{p}", [128, 1024], BF16) for p in range(2)]

                with nc.Block("b4") as bl:

                    @bl.sync
                    def _(sp):
                        for it in range(NTT):
                            if it >= 2:
                                sp.wait_ge(s_dout, it - 1)
                            sp.dma_start(out=x2r[it % 2][:, :],
                                         in_=x2_dr[it * 128:(it + 1) * 128, :]).then_inc(s_x2r, 16)
                            sp.wait_ge(s_dout, it + 1)
                            sp.dma_start(out=out_d[it * 128:(it + 1) * 128, :],
                                         in_=out_sb[it % 2][:, :]).then_inc(s_out2[it % 2], 16)
                        sp.wait_ge(s_out2[0], 16 * (NTT // 2))
                        sp.wait_ge(s_out2[1], 16 * (NTT // 2))

                    @bl.gpsimd
                    def _(g):
                        for sg in range(4):
                            if sg >= 1:
                                g.wait_ge(s_pe3tr, 32 * sg)
                            for o in range(KD):
                                g.dma_start(out=fft[o][:, :],
                                            in_=ffn_dr[o * 128:(o + 1) * 128,
                                                       sg * 512:(sg + 1) * 512]).then_inc(s_ffr, 16)

                    @bl.tensor
                    def _(t):
                        for sg in range(4):
                            t.wait_ge(s_ffr, 128 * (sg + 1))
                            for qq in range(4):
                                tq = 4 * sg + qq
                                if tq >= 2:
                                    t.wait_ge(s_a3te, tq - 1)
                                for n in range(8):
                                    nn = qq * 8 + n
                                    o = nn // 4
                                    j = nn % 4
                                    t.transpose(
                                        psTT[tq % 2][:, n * 128:(n + 1) * 128],
                                        fft[o][:, j * 128:(j + 1) * 128],
                                        ident_b[:, :]).then_inc(s_pe3tr, 1)

                    @bl.scalar
                    def _(sc):
                        ag = [0]

                        def aguard():
                            ag[0] += 1
                            sc.drain().then_inc(sga, 1)
                            sc.wait_ge(sga, ag[0])

                        for sg in range(4):
                            for qq in range(4):
                                tq = 4 * sg + qq
                                sc.wait_ge(s_pe3tr, 8 * (tq + 1))
                                if sg >= 1 and qq == 0:
                                    sc.wait_ge(s_lnsum, 12 + sg)
                                    sc.wait_ge(s_lnnorm, 48 + 4 * sg)
                                for n in range(8):
                                    nn = qq * 8 + n
                                    o = nn // 4
                                    j = nn % 4
                                    ins = sc.copy(
                                        ffn_tok[j][:, o * 128:(o + 1) * 128],
                                        psTT[tq % 2][:, n * 128:(n + 1) * 128])
                                ins.then_inc(s_a3te, 1)
                            gidx = 12 + sg
                            sc.wait_ge(s_lnsum, gidx + 1)
                            for j in range(4):
                                it = 4 * sg + j
                                ins = sc.activation(sq_scr[:, :], ffn_tok[j][:, :],
                                                    AF.Square,
                                                    accum_out=sumsqs[:, it:it + 1])
                            ins.then_inc(s_lnsq, 1)
                            sc.wait_ge(s_lnvar, gidx + 1)
                            c0, c1 = 4 * sg, 4 * sg + 4
                            sc.activation(varb[:, c0:c1], varb[:, c0:c1], AF.Ln,
                                          bias=eps_t[:, :])
                            aguard()
                            sc.activation(rstdb[:, c0:c1], varb[:, c0:c1],
                                          AF.Exp, scale=-0.5).then_inc(s_lnrstd, 1)

                    @bl.vector
                    def _(v):
                        vg = [0]

                        def vguard():
                            vg[0] += 1
                            v.drain().then_inc(sgv, 1)
                            v.wait_ge(sgv, vg[0])

                        for sg in range(4):
                            v.wait_ge(s_a3te, 4 * (sg + 1))
                            for j in range(4):
                                it = 4 * sg + j
                                ins = v.tensor_reduce(sums[:, it:it + 1],
                                                      ffn_tok[j][:, :],
                                                      mybir.AxisListType.X, ALU.add)
                            ins.then_inc(s_lnsum, 1)
                            gidx = 12 + sg
                            v.wait_ge(s_lnsq, gidx + 1)
                            c0, c1 = 4 * sg, 4 * sg + 4
                            vguard()
                            v.tensor_scalar(meanb[:, c0:c1], sums[:, c0:c1],
                                            1.0 / D, None, ALU.mult)
                            vguard()
                            v.tensor_tensor(m2b[:, c0:c1], meanb[:, c0:c1],
                                            meanb[:, c0:c1], ALU.mult)
                            vguard()
                            v.scalar_tensor_tensor(varb[:, c0:c1],
                                                   sumsqs[:, c0:c1],
                                                   1.0 / D, m2b[:, c0:c1],
                                                   ALU.mult, ALU.subtract
                                                   ).then_inc(s_lnvar, 1)
                            v.wait_ge(s_lnrstd, gidx + 1)
                            for j in range(4):
                                it = 4 * sg + j
                                v.tensor_scalar(ln4b[:, :], ffn_tok[j][:, :],
                                                meanb[:, it:it + 1],
                                                rstdb[:, it:it + 1],
                                                ALU.subtract,
                                                ALU.mult).then_inc(s_lnnorm, 1)
                                v.wait_ge(s_x2r, 16 * (it + 1))
                                if it >= 2:
                                    v.wait_ge(s_out2[it % 2], 16 * (it // 2))
                                v.tensor_tensor(out_sb[it % 2][:, :],
                                                x2r[it % 2][:, :],
                                                ln4b[:, :], ALU.add).then_inc(s_dout, 1)

    return nc


def _pretile(w, n_o, n_k):
    return np.ascontiguousarray(
        w.reshape(n_o, 128, n_k, 128).transpose(0, 3, 2, 1).reshape(n_o, 128, n_k * 128))


def kernel(L0_attn_x, L0_attn_W_fwd, L0_attn_W_out, L0_ffn_w1_w, L0_ffn_w1_s,
           L0_ffn_w2_w, L0_ffn_w2_s, L0_ffn_h1_w, L0_ffn_h1_s, L0_ffn_h2_w,
           L0_ffn_h2_s, L0_ffn_w3_w, L0_ffn_w3_s):
    x = np.asarray(L0_attn_x, np.float32)
    wfwd = _pretile(np.asarray(L0_attn_W_fwd, np.float32), 3 * KH, KD)
    wout = _pretile(np.asarray(L0_attn_W_out, np.float32), KD, KH)
    w1 = _pretile(np.asarray(L0_ffn_w1_w, np.float32), KDH, KD)
    w2 = _pretile(np.asarray(L0_ffn_w2_w, np.float32), KDH, KD)
    h1 = _pretile(np.asarray(L0_ffn_h1_w, np.float32), KDH, KDH)
    h2 = _pretile(np.asarray(L0_ffn_h2_w, np.float32), KDH, KDH)
    w3 = _pretile(np.asarray(L0_ffn_w3_w, np.float32), KD, KDH)
    s1c = np.ascontiguousarray(np.asarray(L0_ffn_w1_s, np.float32).reshape(KDH, 128).T)
    s2c = np.ascontiguousarray(np.asarray(L0_ffn_w2_s, np.float32).reshape(KDH, 128).T)
    h1c = np.ascontiguousarray(np.asarray(L0_ffn_h1_s, np.float32).reshape(KDH, 128).T)
    h2c = np.ascontiguousarray(np.asarray(L0_ffn_h2_s, np.float32).reshape(KDH, 128).T)
    w3c = np.ascontiguousarray(np.asarray(L0_ffn_w3_s, np.float32).reshape(KD, 128).T)

    nc = build_nc()
    in_maps = []
    for c in range(NCORE):
        b, q = c // 4, c % 4
        cg = np.zeros((128, 16), np.float32)
        for r in range(NCORE):
            gr = 1.0 if (4 * b <= r < c) else 0.0
            cg[:, r] = gr
            cg[:, 8 + r] = 1.0 - gr
        in_maps.append({
            "xs": np.ascontiguousarray(x[b, q * TPC:(q + 1) * TPC, :]),
            "wfwd": wfwd, "wout": wout, "w1": w1, "w2": w2, "h1": h1,
            "h2": h2, "w3": w3, "s1c": s1c, "s2c": s2c, "h1c": h1c,
            "h2c": h2c, "w3c": w3c, "cg": cg,
        })
    res = run_bass_kernel_spmd(nc, in_maps, core_ids=list(range(NCORE)))
    out = np.empty((B, S, D), np.float32)
    for c in range(NCORE):
        b, q = c // 4, c % 4
        out[b, q * TPC:(q + 1) * TPC, :] = np.asarray(res.results[c]["out"],
                                                      np.float32)
    return out


if __name__ == "__main__":
    import reference as _r
    inp = _r.setup_inputs()
    inp = {k: np.asarray(v) for k, v in inp.items()}
    got = kernel(**inp)
    exp = np.asarray(_r.reference(**inp))
    err = got - exp
    rel = float(np.sqrt((err ** 2).mean() / (exp ** 2).mean()))
    print("rel_l2:", rel, "max:", float(np.abs(err).max()))


# version 10
# speedup vs baseline: 1.0877x; 1.0877x over previous
import numpy as np
import ml_dtypes
import concourse.bass as bass
import concourse.mybir as mybir
from concourse.bass_utils import run_bass_kernel_spmd
from contextlib import ExitStack

F32 = mybir.dt.float32
F32R = mybir.dt.float32r
BF16 = mybir.dt.bfloat16
AF = mybir.ActivationFunctionType
ALU = mybir.AluOpType

B, S, D = 2, 8192, 1024
HID = 2048
DH = 4096
NCORE = 8
TPC = 2048
NTT = TPC // 128
NS = TPC // 512
KD = D // 128
KH = HID // 128
KDH = DH // 128
HT = TPC // 2


def build_nc(dbg=False):
    nc = bass.Bass("TRN2", target_bir_lowering=False, debug=False,
                   detect_race_conditions=False)
    dk = {"kind": "ExternalOutput"} if dbg else {}

    xs_d = nc.dram_tensor("xs", [TPC, D], F32, kind="ExternalInput")
    wfwd_d = nc.dram_tensor("wfwd", [3 * KH, 128, D], F32R, kind="ExternalInput")
    wout_d = nc.dram_tensor("wout", [KD, 128, HID], F32, kind="ExternalInput")
    w1_d = nc.dram_tensor("w1", [KDH, 128, D], F32, kind="ExternalInput")
    w2_d = nc.dram_tensor("w2", [KDH, 128, D], F32, kind="ExternalInput")
    h1_d = nc.dram_tensor("h1", [KDH, 128, DH], F32, kind="ExternalInput")
    h2_d = nc.dram_tensor("h2", [KDH, 128, DH], F32, kind="ExternalInput")
    w3_d = nc.dram_tensor("w3", [KD, 128, DH], F32, kind="ExternalInput")
    s1_d = nc.dram_tensor("s1c", [128, KDH], F32, kind="ExternalInput")
    s2_d = nc.dram_tensor("s2c", [128, KDH], F32, kind="ExternalInput")
    h1c_d = nc.dram_tensor("h1c", [128, KDH], F32, kind="ExternalInput")
    h2c_d = nc.dram_tensor("h2c", [128, KDH], F32, kind="ExternalInput")
    w3c_d = nc.dram_tensor("w3c", [128, KD], F32, kind="ExternalInput")
    cg_d = nc.dram_tensor("cg", [128, 16], F32, kind="ExternalInput")
    out_d = nc.dram_tensor("out", [TPC, D], F32, kind="ExternalOutput")

    hhat_dr = nc.dram_tensor("hhat_dr", [HID, TPC], BF16, **dk)
    ahat_dr = nc.dram_tensor("ahat_dr", [HID, TPC], BF16, **dk)
    x2_dr = nc.dram_tensor("x2_dr", [TPC, D], F32, **dk)
    xnt_dr = nc.dram_tensor("xnt_dr", [D, TPC], BF16, **dk)
    ffn_dr = nc.dram_tensor("ffn_dr", [D, TPC], BF16, **dk)
    g1_dr = nc.dram_tensor("g1_dr", [DH, TPC], BF16)
    g2_dr = nc.dram_tensor("g2_dr", [DH, TPC], BF16)
    contrib = nc.dram_tensor("contrib", [256, 16], F32)
    gath = nc.dram_tensor("gath", [2048, 16], F32)
    if dbg:
        x1n_dbg = nc.dram_tensor("x1n_dbg", [D, TPC], F32R, kind="ExternalOutput")
        u_dbg = nc.dram_tensor("u_dbg", [128, TPC], F32, kind="ExternalOutput")
        w_dbg = nc.dram_tensor("w_dbg", [128, TPC], F32, kind="ExternalOutput")
        f_dbg = nc.dram_tensor("f_dbg", [128, TPC], F32, kind="ExternalOutput")
        v_dbg = nc.dram_tensor("v_dbg", [128, TPC], F32, kind="ExternalOutput")
        st_dbg = nc.dram_tensor("st_dbg", [128, TPC], F32, kind="ExternalOutput")
        hp_dbg = nc.dram_tensor("hp_dbg", [128, TPC], F32, kind="ExternalOutput")

    with ExitStack() as OUT:
        sb = lambda n, shp, dt: OUT.enter_context(nc.sbuf_tensor(n, shp, dt))
        sem = lambda n: OUT.enter_context(nc.semaphore(n))

        ident_b = sb("ident_b", [128, 128], BF16)
        eps_t = sb("eps_t", [128, 1], F32)
        half_t = sb("half_t", [128, 1], F32)
        one_t = sb("one_t", [128, 1], F32)
        nc.const_aps.register(F32, 0.5, half_t[:, 0:1])
        nc.const_aps.register(F32, 1.0, one_t[:, 0:1])
        nc.const_aps.register(F32, 1e-5, eps_t[:, 0:1])
        sums = sb("sums", [128, 16], F32)
        sumsqs = sb("sumsqs", [128, 16], F32)
        meanb = sb("meanb", [128, 16], F32)
        m2b = sb("m2b", [128, 16], F32)
        varb = sb("varb", [128, 16], F32)
        rstdb = sb("rstdb", [128, 16], F32)
        sq_scr = sb("sq_scr", [128, 1024], F32)
        Hc = sb("Hc", [128, 16], F32)
        s1_sb = sb("s1_sb", [128, KDH], F32)
        s2_sb = sb("s2_sb", [128, KDH], F32)
        h1c_sb = sb("h1c_sb", [128, KDH], F32)
        h2c_sb = sb("h2c_sb", [128, KDH], F32)
        w3c_sb = sb("w3c_sb", [128, KD], F32)

        s_setup = sem("s_setup")
        sgv = sem("sgv")
        sga = sem("sga")
        s_xld = sem("s_xld")
        s_lnsum = sem("s_lnsum")
        s_lnsq = sem("s_lnsq")
        s_lnvar = sem("s_lnvar")
        s_lnrstd = sem("s_lnrstd")
        s_lnnorm = sem("s_lnnorm")
        s_p1tr = sem("s_p1tr")
        s_p1evac = sem("s_p1evac")
        s_wg2 = [sem(f"s_wg2{i}") for i in range(2)]
        s_xt = [sem(f"s_xt{i}") for i in range(4)]
        s_pemm = sem("s_pemm")
        s_aevac = sem("s_aevac")
        s_uw = sem("s_uw")
        s_sgm = sem("s_sgm")
        s_di = sem("s_di")
        s_pv = sem("s_pv")
        s_dst = sem("s_dst")
        s_actr = sem("s_actr")
        s_delem = sem("s_delem")
        s_dscan = sem("s_dscan")
        s_spill2 = [sem(f"s_spill2{i}") for i in range(2)]
        s_ctb = sem("s_ctb")
        s_cc = sem("s_cc")
        s_gth = sem("s_gth")
        s_H = sem("s_H")
        s_wout2 = [sem(f"s_wout2{i}") for i in range(2)]
        s_pewo = sem("s_pewo")
        phA = sem("phA")
        phB = sem("phB")
        s_hh2 = [sem(f"s_hh2{i}") for i in range(3)]
        s_dapply = sem("s_dapply")
        s_awevac = sem("s_awevac")
        s_petr = sem("s_petr")
        s_atrevac = sem("s_atrevac")
        s_dx2 = sem("s_dx2")
        s_xsp = [sem(f"s_xsp{i}") for i in range(4)]
        s_x2l = [sem(f"s_x2l{i}") for i in range(3)]
        s_xr2 = [sem(f"s_xr2{i}") for i in range(4)]
        s_pelntr = sem("s_pelntr")
        s_alnevac = sem("s_alnevac")
        s_xnsp = sem("s_xnsp")
        s_xh = sem("s_xh")
        s_wf2 = [sem(f"s_wf2{i}") for i in range(2)]
        s_pef = sem("s_pef")
        s_sp1 = [sem(f"s_sp1{i}") for i in range(2)]
        s_sp2 = [sem(f"s_sp2{i}") for i in range(2)]
        s_grd = sem("s_grd")
        s_afe = sem("s_afe")
        s_dh = sem("s_dh")
        s_w3f2 = [sem(f"s_w3f2{i}") for i in range(2)]
        s_pe3 = sem("s_pe3")
        s_a3e = sem("s_a3e")
        s_pe3tr = sem("s_pe3tr")
        s_a3te = sem("s_a3te")
        s_x2r = sem("s_x2r")
        s_dout = sem("s_dout")
        s_out2 = [sem(f"s_out2{i}") for i in range(2)]
        s_fsp2 = [sem(f"s_fsp2{i}") for i in range(2)]
        s_ffr = sem("s_ffr")
        phC = sem("phC")

        with ExitStack() as SA:
            sbA = lambda n, shp, dt: SA.enter_context(nc.sbuf_tensor(n, shp, dt))
            psA = lambda n, shp, dt: SA.enter_context(nc.psum_tensor(n, shp, dt))
            iota_f = sbA("iota_f", [128, 128], F32)
            iota_p = sbA("iota_p", [128, 1], F32)
            ident_r = sbA("ident_r", [128, 128], F32)
            gA_sb = sbA("gA_sb", [128, 128], F32)
            gB_sb = sbA("gB_sb", [128, 128], F32)
            tmpA = sbA("tmpA", [128, 16], F32)
            tmpB = sbA("tmpB", [128, 16], F32)
            cg_sb = sbA("cg_sb", [128, 16], F32)
            btot = sbA("btot", [128, 16], F32)
            atot = sbA("atot", [128, 16], F32)
            xbuf = [sbA(f"xbuf{i}", [128, D], F32) for i in range(4)]
            xn = [sbA(f"xn{i}", [128, D], F32) for i in range(2)]
            x1nT = [sbA(f"x1nT{j}", [128, TPC], F32R) for j in range(KD)]
            u_sl = [sbA(f"u_sl{p}", [128, TPC], F32) for p in range(2)]
            w_sl = [sbA(f"w_sl{p}", [128, TPC], F32) for p in range(2)]
            hp_sl = sbA("hp_sl", [128, TPC], F32)
            eh_sl = sbA("eh_sl", [128, TPC], F32)
            st_sl = sbA("st_sl", [128, TPC], F32)
            f_sl = sbA("f_sl", [128, TPC], F32)
            v_sl = sbA("v_sl", [128, TPC], F32)
            hhat_sb = [sbA(f"hhat_sb{p}", [128, TPC], BF16) for p in range(2)]
            ahat_sb = [sbA(f"ahat_sb{p}", [128, TPC], BF16) for p in range(2)]
            wslab = [[sbA(f"wsl{p}_{g}", [128, D], F32R) for g in range(3)]
                     for p in range(2)]
            psP = [psA(f"psP{p}", [128, 512], F32) for p in range(2)]
            psG = [[psA(f"psG{g}_{p}", [128, 512], F32) for p in range(2)]
                   for g in range(3)]

            with nc.Block("b1") as bl:

                @bl.sync
                def _(sp):
                    sp.dma_start(out=cg_sb[:, :], in_=cg_d[:, :]).then_inc(s_xld, 16)
                    sp.dma_start(out=s1_sb[:, :], in_=s1_d[:, :]).then_inc(s_xld, 16)
                    sp.dma_start(out=s2_sb[:, :], in_=s2_d[:, :]).then_inc(s_xld, 16)
                    sp.dma_start(out=h1c_sb[:, :], in_=h1c_d[:, :]).then_inc(s_xld, 16)
                    sp.dma_start(out=h2c_sb[:, :], in_=h2c_d[:, :]).then_inc(s_xld, 16)
                    sp.dma_start(out=w3c_sb[:, :], in_=w3c_d[:, :]).then_inc(s_xld, 16)
                    for k in range(2):
                        for g3 in range(3):
                            sp.dma_start(out=wslab[k % 2][g3][:, :],
                                         in_=wfwd_d[g3 * KH + k]).then_inc(s_wg2[k % 2], 16)
                    for i in range(NTT):
                        if i >= 4:
                            sp.wait_ge(s_lnnorm, i - 3)
                        sp.dma_start(out=xbuf[i % 4][:, :],
                                     in_=xs_d[i * 128:(i + 1) * 128, :]).then_inc(s_xt[i % 4], 16)
                    for k in range(2, KH):
                        sp.wait_ge(s_pemm, 4 * (k - 1))
                        for g3 in range(3):
                            sp.dma_start(out=wslab[k % 2][g3][:, :],
                                         in_=wfwd_d[g3 * KH + k]).then_inc(s_wg2[k % 2], 16)
                    sp.sem_inc(phA, 1)
                    sp.wait_ge(phA, 5)

                @bl.gpsimd
                def _(g):
                    g.memset(eps_t[:, :], 1e-5)
                    g.memset(half_t[:, :], 0.5)
                    g.memset(one_t[:, :], 1.0)
                    g.iota(iota_f[:, :], [[1, 128]], channel_multiplier=0,
                           allow_small_or_imprecise_dtypes=True)
                    g.iota(iota_p[:, :], [[1, 1]], channel_multiplier=1,
                           allow_small_or_imprecise_dtypes=True)
                    g.drain().then_inc(s_setup, 1)
                    def _spill_ha(kk):
                        g.wait_ge(s_dscan, kk + 1)
                        g.dma_start(out=hhat_dr[kk * 128:(kk + 1) * 128, :],
                                    in_=hhat_sb[kk % 2][:, :]).then_inc(s_spill2[kk % 2], 16)
                        g.dma_start(out=ahat_dr[kk * 128:(kk + 1) * 128, :],
                                    in_=ahat_sb[kk % 2][:, :]).then_inc(s_spill2[kk % 2], 16)
                    for k in range(KH):
                        g.wait_ge(s_di, k + 1)
                        g.tensor_tensor(v_sl[:, :], u_sl[k % 2][:, :], hp_sl[:, :],
                                        ALU.mult).then_inc(s_pv, 1)
                        g.sem_inc(s_delem, 1)
                        if k >= 1:
                            _spill_ha(k - 1)
                    _spill_ha(KH - 1)
                    if dbg:
                        for jb in range(KD):
                            g.dma_start(out=x1n_dbg[jb * 128:(jb + 1) * 128, :],
                                        in_=x1nT[jb][:, :]).then_inc(s_ctb, 16)
                        for dbt, dbs in [(u_dbg, u_sl), (w_dbg, w_sl),
                                         (f_dbg, f_sl), (v_dbg, v_sl),
                                         (st_dbg, st_sl), (hp_dbg, hp_sl)]:
                            g.dma_start(out=dbt[:, :], in_=dbs[:, :]).then_inc(s_ctb, 16)
                        g.wait_ge(s_ctb, 16 * 14)
                    g.dma_start(out=contrib[0:128, :], in_=atot[:, :]).then_inc(s_ctb, 16)
                    g.dma_start(out=contrib[128:256, :], in_=btot[:, :]).then_inc(s_ctb, 16)
                    g.wait_ge(s_ctb, (16 * 14 if dbg else 0) + 32)
                    g.collective_compute(
                        "AllGather", ALU.bypass,
                        replica_groups=[list(range(NCORE))],
                        ins=[contrib.ap().opt()],
                        outs=[gath.ap().opt()],
                    ).then_inc(s_cc, 1)
                    g.wait_ge(s_cc, 1)
                    g.dma_start(out=gA_sb[:, :],
                                in_=bass.AP(gath, 0, [[16, 128], [4096, 8],
                                                      [1, 16]])).then_inc(s_gth, 16)
                    g.dma_start(out=gB_sb[:, :],
                                in_=bass.AP(gath, 2048, [[16, 128], [4096, 8],
                                                         [1, 16]])).then_inc(s_gth, 16)
                    g.wait_ge(s_spill2[0], 32 * (KH // 2))
                    g.wait_ge(s_spill2[1], 32 * (KH // 2))
                    g.sem_inc(phA, 1)
                    g.wait_ge(phA, 5)

                @bl.vector
                def _(v):
                    vg = [0]

                    def vguard():
                        vg[0] += 1
                        v.drain().then_inc(sgv, 1)
                        v.wait_ge(sgv, vg[0])

                    v.wait_ge(s_setup, 1)
                    v.tensor_scalar(ident_b[:, :], iota_f[:, :], iota_p[:, :], None,
                                    ALU.is_equal)
                    v.memset(Hc[:, :], 0.0).then_inc(s_setup, 1)
                    for grp in range(4):
                        for j in range(4):
                            i = 4 * grp + j
                            v.wait_ge(s_xt[i % 4], 16 * (i // 4 + 1))
                            ins = v.tensor_reduce(sums[:, i:i + 1], xbuf[i % 4][:, :],
                                                  mybir.AxisListType.X, ALU.add)
                        ins.then_inc(s_lnsum, 1)
                        v.wait_ge(s_lnsq, grp + 1)
                        c0, c1 = 4 * grp, 4 * grp + 4
                        vguard()
                        v.tensor_scalar(meanb[:, c0:c1], sums[:, c0:c1],
                                        1.0 / D, None, ALU.mult)
                        vguard()
                        v.tensor_tensor(m2b[:, c0:c1], meanb[:, c0:c1],
                                        meanb[:, c0:c1], ALU.mult)
                        vguard()
                        v.scalar_tensor_tensor(varb[:, c0:c1], sumsqs[:, c0:c1],
                                               1.0 / D, m2b[:, c0:c1],
                                               ALU.mult, ALU.subtract).then_inc(s_lnvar, 1)
                        v.wait_ge(s_lnrstd, grp + 1)
                        for j in range(4):
                            i = 4 * grp + j
                            if i >= 2:
                                v.wait_ge(s_p1tr, 8 * (i - 1))
                            v.tensor_scalar(xn[i % 2][:, :], xbuf[i % 4][:, :],
                                            meanb[:, i:i + 1], rstdb[:, i:i + 1],
                                            ALU.subtract, ALU.mult).then_inc(s_lnnorm, 1)
                    for k in range(KH):
                        v.wait_ge(s_uw, 4 * (k + 1))
                        v.scalar_tensor_tensor(st_sl[:, :], u_sl[k % 2][:, :], 2.0,
                                               w_sl[k % 2][:, :],
                                               ALU.add, ALU.add).then_inc(s_dst, 1)
                        v.wait_ge(s_actr, k + 1)
                        v.scalar_tensor_tensor(f_sl[:, :], w_sl[k % 2][:, :], 1.0,
                                               st_sl[:, :], ALU.add, ALU.mult)
                        v.scalar_tensor_tensor(u_sl[k % 2][:, :], u_sl[k % 2][:, :],
                                               1.0, st_sl[:, :],
                                               ALU.add, ALU.mult)
                        v.wait_ge(s_sgm, k + 1)
                        v.tensor_tensor(hp_sl[:, :], hp_sl[:, :], eh_sl[:, :],
                                        ALU.max).then_inc(s_di, 1)
                        v.wait_ge(s_pv, k + 1)
                        if k >= 2:
                            v.wait_ge(s_spill2[k % 2], 32 * (k // 2))
                        v.tensor_tensor_scan(hhat_sb[k % 2][:, :], f_sl[:, :],
                                             v_sl[:, :], 0.0, ALU.mult, ALU.add)
                        v.tensor_tensor_scan(ahat_sb[k % 2][:, :], f_sl[:, :],
                                             f_sl[:, :], 1.0, ALU.mult, ALU.bypass)
                        vguard()
                        v.tensor_copy(btot[:, k:k + 1], hhat_sb[k % 2][:, TPC - 1:TPC])
                        v.tensor_copy(atot[:, k:k + 1],
                                      ahat_sb[k % 2][:, TPC - 1:TPC]).then_inc(s_dscan, 1)
                    v.wait_ge(s_xld, 96)
                    v.wait_ge(s_gth, 32)
                    for r in range(NCORE):
                        v.tensor_scalar(tmpA[:, :], gA_sb[:, r * 16:(r + 1) * 16],
                                        cg_sb[:, r:r + 1], cg_sb[:, 8 + r:9 + r],
                                        ALU.mult, ALU.add)
                        vguard()
                        v.tensor_tensor(Hc[:, :], Hc[:, :], tmpA[:, :], ALU.mult)
                        v.tensor_scalar(tmpB[:, :], gB_sb[:, r * 16:(r + 1) * 16],
                                        cg_sb[:, r:r + 1], None, ALU.mult)
                        vguard()
                        ins = v.tensor_tensor(Hc[:, :], Hc[:, :], tmpB[:, :], ALU.add)
                    ins.then_inc(s_H, 1)
                    v.drain().then_inc(phA, 1)
                    v.wait_ge(phA, 5)

                @bl.scalar
                def _(sc):
                    ag = [0]

                    def aguard():
                        ag[0] += 1
                        sc.drain().then_inc(sga, 1)
                        sc.wait_ge(sga, ag[0])

                    sc.wait_ge(s_setup, 2)
                    sc.activation(ident_r[:, :], ident_b[:, :],
                                  AF.Copy).then_inc(s_setup, 1)
                    for grp in range(4):
                        for j in range(4):
                            i = 4 * grp + j
                            sc.wait_ge(s_xt[i % 4], 16 * (i // 4 + 1))
                            ins = sc.activation(sq_scr[:, :], xbuf[i % 4][:, :],
                                                AF.Square,
                                                accum_out=sumsqs[:, i:i + 1])
                        ins.then_inc(s_lnsq, 1)
                        sc.wait_ge(s_lnvar, grp + 1)
                        c0, c1 = 4 * grp, 4 * grp + 4
                        sc.activation(varb[:, c0:c1], varb[:, c0:c1], AF.Ln,
                                      bias=eps_t[:, :])
                        aguard()
                        sc.activation(rstdb[:, c0:c1], varb[:, c0:c1], AF.Exp,
                                      scale=-0.5).then_inc(s_lnrstd, 1)
                        for j in range(4):
                            i = 4 * grp + j
                            for half in range(2):
                                q = 2 * i + half
                                sc.wait_ge(s_p1tr, 4 * (q + 1))
                                for jj in range(4):
                                    jb = half * 4 + jj
                                    sc.copy(x1nT[jb][:, i * 128:(i + 1) * 128],
                                            psP[q % 2][:, jj * 128:(jj + 1) * 128]
                                            ).then_inc(s_p1evac, 1)
                    for k in range(KH):
                        def _uw(s):
                            gidx = 4 * k + s
                            par = gidx % 2
                            sc.wait_ge(s_pemm, gidx + 1)
                            cs = slice(s * 512, (s + 1) * 512)
                            sc.activation(u_sl[k % 2][:, cs], psG[0][par][:, :],
                                          AF.Exp, scale=-1.0)
                            sc.activation(w_sl[k % 2][:, cs], psG[1][par][:, :],
                                          AF.Exp, scale=-1.0).then_inc(s_uw, 1)

                        def _ehhp(s):
                            gidx = 4 * k + s
                            par = gidx % 2
                            cs = slice(s * 512, (s + 1) * 512)
                            sc.activation(eh_sl[:, cs], psG[2][par][:, :], AF.Exp,
                                          scale=-1.0)
                            sc.activation(hp_sl[:, cs], psG[2][par][:, :], AF.Copy,
                                          bias=0.5).then_inc(s_aevac, 1)

                        if k >= 2:
                            sc.wait_ge(s_di, k - 1)
                            sc.wait_ge(s_delem, k - 1)
                        _uw(0)
                        _uw(1)
                        if k >= 1:
                            sc.wait_ge(s_delem, k)
                        _ehhp(0)
                        _uw(2)
                        _ehhp(1)
                        _uw(3)
                        _ehhp(2)
                        _ehhp(3)
                        sc.activation(eh_sl[:, :], eh_sl[:, :], AF.Ln,
                                      bias=one_t[:, :])
                        sc.activation(eh_sl[:, :], eh_sl[:, :], AF.Exp,
                                      scale=-1.0).then_inc(s_sgm, 1)
                        sc.wait_ge(s_dst, k + 1)
                        sc.activation(st_sl[:, :], st_sl[:, :], AF.Ln)
                        sc.activation(st_sl[:, :], st_sl[:, :], AF.Exp,
                                      scale=-1.0).then_inc(s_actr, 1)
                    sc.drain().then_inc(phA, 1)
                    sc.wait_ge(phA, 5)

                @bl.tensor
                def _(t):
                    t.wait_ge(s_setup, 3)
                    for i in range(NTT):
                        t.wait_ge(s_lnnorm, i + 1)
                        for jb in range(KD):
                            q = 2 * i + (jb // 4)
                            if jb % 4 == 0 and q >= 2:
                                t.wait_ge(s_p1evac, 4 * (q - 1))
                            t.transpose(psP[q % 2][:, (jb % 4) * 128:(jb % 4 + 1) * 128],
                                        xn[i % 2][:, jb * 128:(jb + 1) * 128],
                                        ident_r[:, :]).then_inc(s_p1tr, 1)
                    for k in range(KH):
                        for s in range(NS):
                            gidx = 4 * k + s
                            par = gidx % 2
                            if gidx == 0:
                                t.wait_ge(s_p1evac, 128)
                            if s == 0:
                                t.wait_ge(s_wg2[k % 2], 48 * (k // 2 + 1))
                            if gidx >= 2:
                                t.wait_ge(s_uw, gidx - 1)
                                t.wait_ge(s_aevac, gidx - 1)
                            for g3 in range(3):
                                for kk in range(KD):
                                    ins = t.matmul(
                                        psG[g3][par][:, :],
                                        wslab[k % 2][g3][:, kk * 128:(kk + 1) * 128],
                                        x1nT[kk][:, s * 512:(s + 1) * 512],
                                        start=(kk == 0), stop=(kk == KD - 1))
                            ins.then_inc(s_pemm, 1)
                    t.drain().then_inc(phA, 1)
                    t.wait_ge(phA, 5)

        with ExitStack() as SB:
            sbB = lambda n, shp, dt: SB.enter_context(nc.sbuf_tensor(n, shp, dt))
            wfb = [sbB(f"wfb{p}", [128, DH], BF16) for p in range(2)]
            x2nT = [sbB(f"x2nT{j}", [128, TPC], BF16) for j in range(KD)]
            with ExitStack() as SD:
                sbD = lambda n, shp, dt: SD.enter_context(nc.sbuf_tensor(n, shp, dt))
                psD = lambda n, shp, dt: SD.enter_context(nc.psum_tensor(n, shp, dt))
                woutb = [sbD(f"woutb{p}", [128, HID], BF16) for p in range(2)]
                hfull = [sbD(f"hfull{k}", [128, TPC], BF16) for k in range(KH)]
                hh_sb = [sbD(f"hh_sb{p}", [128, TPC], BF16) for p in range(3)]
                ah_sb = [sbD(f"ah_sb{p}", [128, TPC], BF16) for p in range(3)]
                mixT = [sbD(f"mixT{o}", [128, 512], BF16) for o in range(KD)]
                mix_tok = [sbD(f"mix_tok{j}", [128, D], BF16) for j in range(4)]
                ln2b = sbD("ln2b", [128, D], F32)
                xbuf2 = [sbD(f"xbuf2{p}", [128, D], F32) for p in range(3)]
                x2buf = [sbD(f"x2buf{j}", [128, D], F32) for j in range(4)]
                x2n = [sbD(f"x2n{p}", [128, D], BF16) for p in range(2)]
                psW = [psD(f"psW{o4}", [128, 512], F32) for o4 in range(4)]
                psT = [psD(f"psT{p}", [128, 1024], BF16) for p in range(2)]
                psL = [psD(f"psL{p}", [128, 1024], BF16) for p in range(2)]

                with nc.Block("b2") as bl:

                    @bl.sync
                    def _(sp):
                        for i in range(NTT):
                            if i >= 3:
                                sp.wait_ge(s_dx2, i - 2)
                            sp.dma_start(out=xbuf2[i % 3][:, :],
                                         in_=xs_d[i * 128:(i + 1) * 128, :]).then_inc(s_x2l[i % 3], 16)
                            sp.wait_ge(s_dx2, i + 1)
                            sp.dma_start(out=x2_dr[i * 128:(i + 1) * 128, :],
                                         in_=x2buf[i % 4][:, :]).then_inc(s_xsp[i % 4], 16)
                        for j in range(4):
                            sp.wait_ge(s_xsp[j], 16 * (NTT // 4))
                        for i in range(NTT):
                            if i >= 4:
                                sp.wait_ge(s_lnnorm, 29 + i)
                            sp.dma_start(out=x2buf[i % 4][:, :],
                                         in_=x2_dr[i * 128:(i + 1) * 128, :]).then_inc(s_xr2[i % 4], 16)
                        sp.sem_inc(phB, 1)
                        sp.wait_ge(phB, 5)

                    @bl.gpsimd
                    def _(g):
                        for ws in range(2):
                            g.dma_start(out=woutb[ws % 2][:, :],
                                        in_=wout_d[ws % KD]).then_inc(s_wout2[ws % 2], 16)
                        g.wait_ge(s_spill2[0], 32 * (KH // 2))
                        g.wait_ge(s_spill2[1], 32 * (KH // 2))
                        for k in range(KH):
                            if k >= 3:
                                g.wait_ge(s_dapply, k - 2)
                            g.dma_start(out=hh_sb[k % 3][:, :],
                                        in_=hhat_dr[k * 128:(k + 1) * 128, :]).then_inc(s_hh2[k % 3], 16)
                            g.dma_start(out=ah_sb[k % 3][:, :],
                                        in_=ahat_dr[k * 128:(k + 1) * 128, :]).then_inc(s_hh2[k % 3], 16)
                        for ws in range(2, 4 * NS * 2):
                            g.wait_ge(s_pewo, ws - 1)
                            g.dma_start(out=woutb[ws % 2][:, :],
                                        in_=wout_d[ws % KD]).then_inc(s_wout2[ws % 2], 16)
                        g.dma_start(out=wfb[0][:, 0:D],
                                    in_=w1_d[0]).then_inc(s_wf2[0], 16)
                        g.dma_start(out=wfb[0][:, D:2 * D],
                                    in_=w2_d[0]).then_inc(s_wf2[0], 16)
                        g.sem_inc(phB, 1)
                        g.wait_ge(phB, 5)

                    @bl.vector
                    def _(v):
                        vg = [0]

                        def vguard():
                            vg[0] += 1
                            v.drain().then_inc(sgv, 1)
                            v.wait_ge(sgv, vg[0])

                        v.wait_ge(s_H, 1)
                        for k in range(KH):
                            v.wait_ge(s_hh2[k % 3], 32 * (k // 3 + 1))
                            v.scalar_tensor_tensor(hfull[k][:, :], ah_sb[k % 3][:, :],
                                                   Hc[:, k:k + 1], hh_sb[k % 3][:, :],
                                                   ALU.mult, ALU.add).then_inc(s_dapply, 1)
                        for s in range(NS):
                            v.wait_ge(s_atrevac, 4 * (s + 1))
                            for j in range(4):
                                i = 4 * s + j
                                ins = v.tensor_reduce(sums[:, i:i + 1],
                                                      mix_tok[j][:, :],
                                                      mybir.AxisListType.X, ALU.add)
                            ins.then_inc(s_lnsum, 1)
                            gidx = 4 + s
                            v.wait_ge(s_lnsq, gidx + 1)
                            c0, c1 = 4 * s, 4 * s + 4
                            vguard()
                            v.tensor_scalar(meanb[:, c0:c1], sums[:, c0:c1],
                                            1.0 / D, None, ALU.mult)
                            vguard()
                            v.tensor_tensor(m2b[:, c0:c1], meanb[:, c0:c1],
                                            meanb[:, c0:c1], ALU.mult)
                            vguard()
                            v.scalar_tensor_tensor(varb[:, c0:c1], sumsqs[:, c0:c1],
                                                   1.0 / D, m2b[:, c0:c1],
                                                   ALU.mult, ALU.subtract
                                                   ).then_inc(s_lnvar, 1)
                            v.wait_ge(s_lnrstd, gidx + 1)
                            if s >= 1:
                                for j in range(4):
                                    v.wait_ge(s_xsp[j], 16 * s)
                            for j in range(4):
                                i = 4 * s + j
                                v.tensor_scalar(ln2b[:, :], mix_tok[j][:, :],
                                                meanb[:, i:i + 1], rstdb[:, i:i + 1],
                                                ALU.subtract,
                                                ALU.mult).then_inc(s_lnnorm, 1)
                                v.wait_ge(s_x2l[i % 3], 16 * (i // 3 + 1))
                                v.tensor_tensor(x2buf[j][:, :], xbuf2[i % 3][:, :],
                                                ln2b[:, :], ALU.add).then_inc(s_dx2, 1)
                        for gi in range(4):
                            for j in range(4):
                                i = 4 * gi + j
                                v.wait_ge(s_xr2[i % 4], 16 * (i // 4 + 1))
                                ins = v.tensor_reduce(sums[:, i:i + 1],
                                                      x2buf[i % 4][:, :],
                                                      mybir.AxisListType.X, ALU.add)
                            ins.then_inc(s_lnsum, 1)
                            gidx = 8 + gi
                            v.wait_ge(s_lnsq, gidx + 1)
                            c0, c1 = 4 * gi, 4 * gi + 4
                            vguard()
                            v.tensor_scalar(meanb[:, c0:c1], sums[:, c0:c1],
                                            1.0 / D, None, ALU.mult)
                            vguard()
                            v.tensor_tensor(m2b[:, c0:c1], meanb[:, c0:c1],
                                            meanb[:, c0:c1], ALU.mult)
                            vguard()
                            v.scalar_tensor_tensor(varb[:, c0:c1], sumsqs[:, c0:c1],
                                                   1.0 / D, m2b[:, c0:c1],
                                                   ALU.mult, ALU.subtract
                                                   ).then_inc(s_lnvar, 1)
                            v.wait_ge(s_lnrstd, gidx + 1)
                            for j in range(4):
                                i = 4 * gi + j
                                if i >= 2:
                                    v.wait_ge(s_pelntr, 8 * (i - 1))
                                v.tensor_scalar(x2n[i % 2][:, :], x2buf[i % 4][:, :],
                                                meanb[:, i:i + 1], rstdb[:, i:i + 1],
                                                ALU.subtract,
                                                ALU.mult).then_inc(s_lnnorm, 1)
                        v.drain().then_inc(phB, 1)
                        v.wait_ge(phB, 5)

                    @bl.scalar
                    def _(sc):
                        ag = [0]

                        def aguard():
                            ag[0] += 1
                            sc.drain().then_inc(sga, 1)
                            sc.wait_ge(sga, ag[0])

                        for s in range(NS):
                            for oh in range(2):
                                idx = 2 * s + oh
                                sc.wait_ge(s_pewo, 4 * (idx + 1))
                                if s >= 1:
                                    sc.wait_ge(s_petr, 8 * (4 * s + 2 * oh - 2))
                                for o4 in range(4):
                                    ins = sc.copy(mixT[oh * 4 + o4][:, :], psW[o4][:, :])
                                ins.then_inc(s_awevac, 1)
                                for half in range(2):
                                    tq = 4 * s + 2 * oh + half
                                    sc.wait_ge(s_petr, 8 * (tq + 1))
                                    if s >= 1 and oh == 0 and half == 0:
                                        sc.wait_ge(s_lnsum, s + 4)
                                        sc.wait_ge(s_lnnorm, 16 + 4 * (s - 1) + 4)
                                    for n in range(8):
                                        nn = half * 8 + n
                                        o4 = nn // 4
                                        j = nn % 4
                                        o = oh * 4 + o4
                                        ins = sc.copy(
                                            mix_tok[j][:, o * 128:(o + 1) * 128],
                                            psT[tq % 2][:, n * 128:(n + 1) * 128])
                                    ins.then_inc(s_atrevac, 1)
                            gidx = 4 + s
                            sc.wait_ge(s_lnsum, gidx + 1)
                            for j in range(4):
                                i = 4 * s + j
                                ins = sc.activation(sq_scr[:, :], mix_tok[j][:, :],
                                                    AF.Square,
                                                    accum_out=sumsqs[:, i:i + 1])
                            ins.then_inc(s_lnsq, 1)
                            sc.wait_ge(s_lnvar, gidx + 1)
                            c0, c1 = 4 * s, 4 * s + 4
                            sc.activation(varb[:, c0:c1], varb[:, c0:c1], AF.Ln,
                                          bias=eps_t[:, :])
                            aguard()
                            sc.activation(rstdb[:, c0:c1], varb[:, c0:c1], AF.Exp,
                                          scale=-0.5).then_inc(s_lnrstd, 1)
                        for gi in range(4):
                            gidx = 8 + gi
                            sc.wait_ge(s_lnsum, gidx + 1)
                            for j in range(4):
                                i = 4 * gi + j
                                sc.wait_ge(s_xr2[i % 4], 16 * (i // 4 + 1))
                                ins = sc.activation(sq_scr[:, :], x2buf[i % 4][:, :],
                                                    AF.Square,
                                                    accum_out=sumsqs[:, i:i + 1])
                            ins.then_inc(s_lnsq, 1)
                            sc.wait_ge(s_lnvar, gidx + 1)
                            c0, c1 = 4 * gi, 4 * gi + 4
                            sc.activation(varb[:, c0:c1], varb[:, c0:c1], AF.Ln,
                                          bias=eps_t[:, :])
                            aguard()
                            sc.activation(rstdb[:, c0:c1], varb[:, c0:c1], AF.Exp,
                                          scale=-0.5).then_inc(s_lnrstd, 1)
                            for j in range(4):
                                i = 4 * gi + j
                                sc.wait_ge(s_pelntr, 8 * (i + 1))
                                for jb in range(KD):
                                    sc.copy(x2nT[jb][:, i * 128:(i + 1) * 128],
                                            psL[i % 2][:, jb * 128:(jb + 1) * 128]
                                            ).then_inc(s_alnevac, 1)
                        sc.drain().then_inc(phB, 1)
                        sc.wait_ge(phB, 5)

                    @bl.tensor
                    def _(t):
                        for s in range(NS):
                            for oh in range(2):
                                idx = 2 * s + oh
                                if idx >= 1:
                                    t.wait_ge(s_awevac, idx)
                                for o4 in range(4):
                                    o = oh * 4 + o4
                                    ws = 8 * s + o
                                    t.wait_ge(s_wout2[ws % 2], 16 * (ws // 2 + 1))
                                    for k in range(KH):
                                        if s == 0 and oh == 0 and o4 == 0:
                                            t.wait_ge(s_dapply, k + 1)
                                        ins = t.matmul(
                                            psW[o4][:, :],
                                            woutb[ws % 2][:, k * 128:(k + 1) * 128],
                                            hfull[k][:, s * 512:(s + 1) * 512],
                                            start=(k == 0), stop=(k == KH - 1))
                                    ins.then_inc(s_pewo, 1)
                                t.wait_ge(s_awevac, idx + 1)
                                for half in range(2):
                                    tq = 4 * s + 2 * oh + half
                                    if tq >= 2:
                                        t.wait_ge(s_atrevac, tq - 1)
                                    for n in range(8):
                                        nn = half * 8 + n
                                        o4 = nn // 4
                                        j = nn % 4
                                        t.transpose(
                                            psT[tq % 2][:, n * 128:(n + 1) * 128],
                                            mixT[oh * 4 + o4][:, j * 128:(j + 1) * 128],
                                            ident_b[:, :]).then_inc(s_petr, 1)
                        for i in range(NTT):
                            t.wait_ge(s_lnnorm, 32 + i + 1)
                            if i >= 2:
                                t.wait_ge(s_alnevac, 8 * (i - 1))
                            for jb in range(KD):
                                t.transpose(psL[i % 2][:, jb * 128:(jb + 1) * 128],
                                            x2n[i % 2][:, jb * 128:(jb + 1) * 128],
                                            ident_b[:, :]).then_inc(s_pelntr, 1)
                        t.drain().then_inc(phB, 1)
                        t.wait_ge(phB, 5)

            with ExitStack() as SC:
                sbC = lambda n, shp, dt: SC.enter_context(nc.sbuf_tensor(n, shp, dt))
                psC = lambda n, shp, dt: SC.enter_context(nc.psum_tensor(n, shp, dt))
                hbig = sbC("hbig", [128, KDH * TPC], BF16)
                ybA = [sbC(f"ybA{p}", [128, TPC], BF16) for p in range(2)]
                ybB = [sbC(f"ybB{p}", [128, TPC], BF16) for p in range(2)]
                fsb = [sbC(f"fsb{p}", [128, 512], BF16) for p in range(2)]
                psF = [psC(f"psF{i}", [128, 512], F32) for i in range(8)]

                cnt = {"wf0": 2, "wf1": 0, "pef": 0, "afe1": 0, "afe2": 0,
                       "dh": 0, "sp1": [0, 0], "a3e": 0, "f3sp": [0, 0]}

                with nc.Block("b3") as bl:

                    @bl.sync
                    def _(sp):
                        sp.sem_inc(phC, 1)
                        sp.wait_ge(phC, 5)

                    @bl.gpsimd
                    def _(g):
                        for o in range(1, KDH):
                            g.wait_ge(s_pef, max(0, 2 * o - 2))
                            g.dma_start(out=wfb[o % 2][:, 0:D],
                                        in_=w1_d[o]).then_inc(s_wf2[o % 2], 16)
                            g.dma_start(out=wfb[o % 2][:, D:2 * D],
                                        in_=w2_d[o]).then_inc(s_wf2[o % 2], 16)
                        def _sp1(o):
                            g.wait_ge(s_afe, 64 + o + 1)
                            g.dma_start(out=g1_dr[o * 128:(o + 1) * 128, :],
                                        in_=ybA[o % 2][:, :]).then_inc(s_sp1[o % 2], 16)
                        for o in range(KDH):
                            g.wait_ge(s_pef, 64 + max(0, o - 1))
                            g.dma_start(out=wfb[o % 2][:, :],
                                        in_=h1_d[o]).then_inc(s_wf2[o % 2], 16)
                            if o >= 2:
                                _sp1(o - 2)
                        _sp1(KDH - 2)
                        _sp1(KDH - 1)
                        g.wait_ge(s_pef, 96)
                        g.wait_ge(s_sp1[0], 256)
                        g.wait_ge(s_sp1[1], 256)
                        g.dma_start(out=hbig[:, :],
                                    in_=bass.AP(g1_dr, 0, [[TPC, 128],
                                                           [128 * TPC, KDH],
                                                           [1, TPC]])).then_inc(s_grd, 16)
                        def _sp2(o):
                            g.wait_ge(s_afe, 96 + o + 1)
                            g.dma_start(out=g2_dr[o * 128:(o + 1) * 128, :],
                                        in_=ybB[o % 2][:, :]).then_inc(s_sp2[o % 2], 16)
                        for o in range(KDH):
                            g.wait_ge(s_pef, 96 + max(0, o - 1))
                            g.dma_start(out=wfb[o % 2][:, :],
                                        in_=h2_d[o]).then_inc(s_wf2[o % 2], 16)
                            if o >= 2:
                                _sp2(o - 2)
                        _sp2(KDH - 2)
                        _sp2(KDH - 1)
                        g.wait_ge(s_pef, 128)
                        g.wait_ge(s_sp2[0], 256)
                        g.wait_ge(s_sp2[1], 256)
                        g.dma_start(out=hbig[:, :],
                                    in_=bass.AP(g2_dr, 0, [[TPC, 128],
                                                           [128 * TPC, KDH],
                                                           [1, TPC]])).then_inc(s_grd, 16)
                        def _sp3(o):
                            for s in range(NS):
                                m = 4 * o + s
                                g.wait_ge(s_a3e, m + 1)
                                g.dma_start(out=ffn_dr[o * 128:(o + 1) * 128,
                                                       s * 512:(s + 1) * 512],
                                            in_=fsb[m % 2][:, :]).then_inc(s_fsp2[m % 2], 16)
                        for o in range(KD):
                            g.wait_ge(s_pef, 128 + max(0, o - 1))
                            g.dma_start(out=wfb[o % 2][:, :],
                                        in_=w3_d[o]).then_inc(s_wf2[o % 2], 16)
                            if o >= 2:
                                _sp3(o - 2)
                        _sp3(KD - 2)
                        _sp3(KD - 1)
                        g.wait_ge(s_fsp2[0], 16 * 16)
                        g.wait_ge(s_fsp2[1], 16 * 16)
                        g.sem_inc(phC, 1)
                        g.wait_ge(phC, 5)

                    @bl.vector
                    def _(v):
                        for o in range(KDH):
                            v.wait_ge(s_afe, 2 * (o + 1))
                            v.tensor_tensor(hbig[:, o * TPC:(o + 1) * TPC],
                                            ybA[o % 2][:, :], ybB[o % 2][:, :],
                                            ALU.mult).then_inc(s_dh, 1)
                        v.drain().then_inc(phC, 1)
                        v.wait_ge(phC, 5)

                    @bl.scalar
                    def _(sc):
                        sc.wait_ge(s_xld, 96)
                        for o in range(KDH):
                            sc.wait_ge(s_pef, 2 * o + 1)
                            if o >= 2:
                                sc.wait_ge(s_dh, o - 1)
                            for s in range(NS):
                                ins = sc.activation(ybA[o % 2][:, s * 512:(s + 1) * 512],
                                                    psF[s][:, :], AF.Silu,
                                                    scale=s1_sb[:, o:o + 1])
                            ins.then_inc(s_afe, 1)
                            sc.wait_ge(s_pef, 2 * o + 2)
                            for s in range(NS):
                                ins = sc.activation(ybB[o % 2][:, s * 512:(s + 1) * 512],
                                                    psF[4 + s][:, :], AF.Copy,
                                                    scale=s2_sb[:, o:o + 1])
                            ins.then_inc(s_afe, 1)
                        for o in range(KDH):
                            sc.wait_ge(s_pef, 64 + o + 1)
                            if o >= 2:
                                sc.wait_ge(s_sp1[o % 2], 16 * (o // 2))
                            for s in range(NS):
                                ins = sc.activation(ybA[o % 2][:, s * 512:(s + 1) * 512],
                                                    psF[4 * (o % 2) + s][:, :],
                                                    AF.Gelu,
                                                    scale=h1c_sb[:, o:o + 1])
                            ins.then_inc(s_afe, 1)
                        for o in range(KDH):
                            sc.wait_ge(s_pef, 96 + o + 1)
                            if o >= 2:
                                sc.wait_ge(s_sp2[o % 2], 16 * (o // 2))
                            for s in range(NS):
                                ins = sc.activation(ybB[o % 2][:, s * 512:(s + 1) * 512],
                                                    psF[4 * (o % 2) + s][:, :],
                                                    AF.Gelu,
                                                    scale=h2c_sb[:, o:o + 1])
                            ins.then_inc(s_afe, 1)
                        for o in range(KD):
                            sc.wait_ge(s_pef, 128 + o + 1)
                            for s in range(NS):
                                m = 4 * o + s
                                if m >= 2:
                                    sc.wait_ge(s_fsp2[m % 2], 16 * (m // 2))
                                sc.activation(fsb[m % 2][:, :],
                                              psF[4 * (o % 2) + s][:, :], AF.Copy,
                                              scale=w3c_sb[:, o:o + 1]
                                              ).then_inc(s_a3e, 1)
                        sc.drain().then_inc(phC, 1)
                        sc.wait_ge(phC, 5)

                    @bl.tensor
                    def _(t):
                        t.wait_ge(s_alnevac, 128)
                        for o in range(KDH):
                            t.wait_ge(s_wf2[o % 2], 32 * (o // 2 + 1))
                            if o >= 2:
                                t.wait_ge(s_afe, 2 * (o - 1))
                            for s in range(NS):
                                for kk in range(KD):
                                    ins = t.matmul(
                                        psF[s][:, :],
                                        wfb[o % 2][:, kk * 128:(kk + 1) * 128],
                                        x2nT[kk][:, s * 512:(s + 1) * 512],
                                        start=(kk == 0), stop=(kk == KD - 1))
                            ins.then_inc(s_pef, 1)
                            if o >= 2:
                                t.wait_ge(s_afe, 2 * o - 1)
                            for s in range(NS):
                                for kk in range(KD):
                                    ins = t.matmul(
                                        psF[4 + s][:, :],
                                        wfb[o % 2][:, D + kk * 128:D + (kk + 1) * 128],
                                        x2nT[kk][:, s * 512:(s + 1) * 512],
                                        start=(kk == 0), stop=(kk == KD - 1))
                            ins.then_inc(s_pef, 1)
                        t.wait_ge(s_dh, KDH)
                        for o in range(KDH):
                            t.wait_ge(s_wf2[o % 2], 16 * (32 + o // 2 + 1))
                            t.wait_ge(s_afe, 63 + o)
                            for s in range(NS):
                                for kk in range(KDH):
                                    ins = t.matmul(
                                        psF[4 * (o % 2) + s][:, :],
                                        wfb[o % 2][:, kk * 128:(kk + 1) * 128],
                                        hbig[:, kk * TPC + s * 512:kk * TPC + (s + 1) * 512],
                                        start=(kk == 0), stop=(kk == KDH - 1))
                            ins.then_inc(s_pef, 1)
                        t.wait_ge(s_grd, 16)
                        for o in range(KDH):
                            t.wait_ge(s_wf2[o % 2], 16 * (48 + o // 2 + 1))
                            t.wait_ge(s_afe, 95 + o)
                            for s in range(NS):
                                for kk in range(KDH):
                                    ins = t.matmul(
                                        psF[4 * (o % 2) + s][:, :],
                                        wfb[o % 2][:, kk * 128:(kk + 1) * 128],
                                        hbig[:, kk * TPC + s * 512:kk * TPC + (s + 1) * 512],
                                        start=(kk == 0), stop=(kk == KDH - 1))
                            ins.then_inc(s_pef, 1)
                        t.wait_ge(s_grd, 32)
                        for o in range(KD):
                            t.wait_ge(s_wf2[o % 2], 16 * (64 + o // 2 + 1))
                            t.wait_ge(s_afe, 127 + (1 if o >= 1 else 0))
                            if o >= 2:
                                t.wait_ge(s_a3e, 4 * (o - 1))
                            for s in range(NS):
                                for kk in range(KDH):
                                    ins = t.matmul(
                                        psF[4 * (o % 2) + s][:, :],
                                        wfb[o % 2][:, kk * 128:(kk + 1) * 128],
                                        hbig[:, kk * TPC + s * 512:kk * TPC + (s + 1) * 512],
                                        start=(kk == 0), stop=(kk == KDH - 1))
                            ins.then_inc(s_pef, 1)
                        t.drain().then_inc(phC, 1)
                        t.wait_ge(phC, 5)


            with ExitStack() as SE:
                sbE = lambda n, shp, dt: SE.enter_context(nc.sbuf_tensor(n, shp, dt))
                psE = lambda n, shp, dt: SE.enter_context(nc.psum_tensor(n, shp, dt))
                fft = [sbE(f"fft{o}", [128, 512], BF16) for o in range(KD)]
                ffn_tok = [sbE(f"ffn_tok{j}", [128, D], BF16) for j in range(4)]
                ln4b = sbE("ln4b", [128, D], BF16)
                x2r = [sbE(f"x2r{p}", [128, D], F32) for p in range(2)]
                out_sb = [sbE(f"out_sb{p}", [128, D], F32) for p in range(2)]
                psTT = [psE(f"psTT{p}", [128, 1024], BF16) for p in range(2)]

                with nc.Block("b4") as bl:

                    @bl.sync
                    def _(sp):
                        for it in range(NTT):
                            if it >= 2:
                                sp.wait_ge(s_dout, it - 1)
                            sp.dma_start(out=x2r[it % 2][:, :],
                                         in_=x2_dr[it * 128:(it + 1) * 128, :]).then_inc(s_x2r, 16)
                            sp.wait_ge(s_dout, it + 1)
                            sp.dma_start(out=out_d[it * 128:(it + 1) * 128, :],
                                         in_=out_sb[it % 2][:, :]).then_inc(s_out2[it % 2], 16)
                        sp.wait_ge(s_out2[0], 16 * (NTT // 2))
                        sp.wait_ge(s_out2[1], 16 * (NTT // 2))

                    @bl.gpsimd
                    def _(g):
                        for sg in range(4):
                            if sg >= 1:
                                g.wait_ge(s_pe3tr, 32 * sg)
                            for o in range(KD):
                                g.dma_start(out=fft[o][:, :],
                                            in_=ffn_dr[o * 128:(o + 1) * 128,
                                                       sg * 512:(sg + 1) * 512]).then_inc(s_ffr, 16)

                    @bl.tensor
                    def _(t):
                        for sg in range(4):
                            t.wait_ge(s_ffr, 128 * (sg + 1))
                            for qq in range(4):
                                tq = 4 * sg + qq
                                if tq >= 2:
                                    t.wait_ge(s_a3te, tq - 1)
                                for n in range(8):
                                    nn = qq * 8 + n
                                    o = nn // 4
                                    j = nn % 4
                                    t.transpose(
                                        psTT[tq % 2][:, n * 128:(n + 1) * 128],
                                        fft[o][:, j * 128:(j + 1) * 128],
                                        ident_b[:, :]).then_inc(s_pe3tr, 1)

                    @bl.scalar
                    def _(sc):
                        ag = [0]

                        def aguard():
                            ag[0] += 1
                            sc.drain().then_inc(sga, 1)
                            sc.wait_ge(sga, ag[0])

                        for sg in range(4):
                            for qq in range(4):
                                tq = 4 * sg + qq
                                sc.wait_ge(s_pe3tr, 8 * (tq + 1))
                                if sg >= 1 and qq == 0:
                                    sc.wait_ge(s_lnsum, 12 + sg)
                                    sc.wait_ge(s_lnnorm, 48 + 4 * sg)
                                for n in range(8):
                                    nn = qq * 8 + n
                                    o = nn // 4
                                    j = nn % 4
                                    ins = sc.copy(
                                        ffn_tok[j][:, o * 128:(o + 1) * 128],
                                        psTT[tq % 2][:, n * 128:(n + 1) * 128])
                                ins.then_inc(s_a3te, 1)
                            gidx = 12 + sg
                            sc.wait_ge(s_lnsum, gidx + 1)
                            for j in range(4):
                                it = 4 * sg + j
                                ins = sc.activation(sq_scr[:, :], ffn_tok[j][:, :],
                                                    AF.Square,
                                                    accum_out=sumsqs[:, it:it + 1])
                            ins.then_inc(s_lnsq, 1)
                            sc.wait_ge(s_lnvar, gidx + 1)
                            c0, c1 = 4 * sg, 4 * sg + 4
                            sc.activation(varb[:, c0:c1], varb[:, c0:c1], AF.Ln,
                                          bias=eps_t[:, :])
                            aguard()
                            sc.activation(rstdb[:, c0:c1], varb[:, c0:c1],
                                          AF.Exp, scale=-0.5).then_inc(s_lnrstd, 1)

                    @bl.vector
                    def _(v):
                        vg = [0]

                        def vguard():
                            vg[0] += 1
                            v.drain().then_inc(sgv, 1)
                            v.wait_ge(sgv, vg[0])

                        for sg in range(4):
                            v.wait_ge(s_a3te, 4 * (sg + 1))
                            for j in range(4):
                                it = 4 * sg + j
                                ins = v.tensor_reduce(sums[:, it:it + 1],
                                                      ffn_tok[j][:, :],
                                                      mybir.AxisListType.X, ALU.add)
                            ins.then_inc(s_lnsum, 1)
                            gidx = 12 + sg
                            v.wait_ge(s_lnsq, gidx + 1)
                            c0, c1 = 4 * sg, 4 * sg + 4
                            vguard()
                            v.tensor_scalar(meanb[:, c0:c1], sums[:, c0:c1],
                                            1.0 / D, None, ALU.mult)
                            vguard()
                            v.tensor_tensor(m2b[:, c0:c1], meanb[:, c0:c1],
                                            meanb[:, c0:c1], ALU.mult)
                            vguard()
                            v.scalar_tensor_tensor(varb[:, c0:c1],
                                                   sumsqs[:, c0:c1],
                                                   1.0 / D, m2b[:, c0:c1],
                                                   ALU.mult, ALU.subtract
                                                   ).then_inc(s_lnvar, 1)
                            v.wait_ge(s_lnrstd, gidx + 1)
                            for j in range(4):
                                it = 4 * sg + j
                                v.tensor_scalar(ln4b[:, :], ffn_tok[j][:, :],
                                                meanb[:, it:it + 1],
                                                rstdb[:, it:it + 1],
                                                ALU.subtract,
                                                ALU.mult).then_inc(s_lnnorm, 1)
                                v.wait_ge(s_x2r, 16 * (it + 1))
                                if it >= 2:
                                    v.wait_ge(s_out2[it % 2], 16 * (it // 2))
                                v.tensor_tensor(out_sb[it % 2][:, :],
                                                x2r[it % 2][:, :],
                                                ln4b[:, :], ALU.add).then_inc(s_dout, 1)

    return nc


def _pretile(w, n_o, n_k):
    return np.ascontiguousarray(
        w.reshape(n_o, 128, n_k, 128).transpose(0, 3, 2, 1).reshape(n_o, 128, n_k * 128))


def kernel(L0_attn_x, L0_attn_W_fwd, L0_attn_W_out, L0_ffn_w1_w, L0_ffn_w1_s,
           L0_ffn_w2_w, L0_ffn_w2_s, L0_ffn_h1_w, L0_ffn_h1_s, L0_ffn_h2_w,
           L0_ffn_h2_s, L0_ffn_w3_w, L0_ffn_w3_s):
    x = np.asarray(L0_attn_x, np.float32)
    wfwd = _pretile(np.asarray(L0_attn_W_fwd, np.float32), 3 * KH, KD)
    wout = _pretile(np.asarray(L0_attn_W_out, np.float32), KD, KH)
    w1 = _pretile(np.asarray(L0_ffn_w1_w, np.float32), KDH, KD)
    w2 = _pretile(np.asarray(L0_ffn_w2_w, np.float32), KDH, KD)
    h1 = _pretile(np.asarray(L0_ffn_h1_w, np.float32), KDH, KDH)
    h2 = _pretile(np.asarray(L0_ffn_h2_w, np.float32), KDH, KDH)
    w3 = _pretile(np.asarray(L0_ffn_w3_w, np.float32), KD, KDH)
    s1c = np.ascontiguousarray(np.asarray(L0_ffn_w1_s, np.float32).reshape(KDH, 128).T)
    s2c = np.ascontiguousarray(np.asarray(L0_ffn_w2_s, np.float32).reshape(KDH, 128).T)
    h1c = np.ascontiguousarray(np.asarray(L0_ffn_h1_s, np.float32).reshape(KDH, 128).T)
    h2c = np.ascontiguousarray(np.asarray(L0_ffn_h2_s, np.float32).reshape(KDH, 128).T)
    w3c = np.ascontiguousarray(np.asarray(L0_ffn_w3_s, np.float32).reshape(KD, 128).T)

    nc = build_nc()
    in_maps = []
    for c in range(NCORE):
        b, q = c // 4, c % 4
        cg = np.zeros((128, 16), np.float32)
        for r in range(NCORE):
            gr = 1.0 if (4 * b <= r < c) else 0.0
            cg[:, r] = gr
            cg[:, 8 + r] = 1.0 - gr
        in_maps.append({
            "xs": np.ascontiguousarray(x[b, q * TPC:(q + 1) * TPC, :]),
            "wfwd": wfwd, "wout": wout, "w1": w1, "w2": w2, "h1": h1,
            "h2": h2, "w3": w3, "s1c": s1c, "s2c": s2c, "h1c": h1c,
            "h2c": h2c, "w3c": w3c, "cg": cg,
        })
    res = run_bass_kernel_spmd(nc, in_maps, core_ids=list(range(NCORE)))
    out = np.empty((B, S, D), np.float32)
    for c in range(NCORE):
        b, q = c // 4, c % 4
        out[b, q * TPC:(q + 1) * TPC, :] = np.asarray(res.results[c]["out"],
                                                      np.float32)
    return out


if __name__ == "__main__":
    import reference as _r
    inp = _r.setup_inputs()
    inp = {k: np.asarray(v) for k, v in inp.items()}
    got = kernel(**inp)
    exp = np.asarray(_r.reference(**inp))
    err = got - exp
    rel = float(np.sqrt((err ** 2).mean() / (exp ** 2).mean()))
    print("rel_l2:", rel, "max:", float(np.abs(err).max()))
